# version 02
# baseline (speedup 1.0000x reference)
import numpy as np
from collections import defaultdict

import concourse.bass as bass
import concourse.mybir as mybir
from concourse.bass_core_rust import IndirectOffsetOnAxis
from concourse.expressions import RuntimeValue
from concourse.bass_utils import run_bass_kernel_spmd

F32 = mybir.dt.float32
BF16 = mybir.dt.bfloat16
OP = mybir.AluOpType
AF = mybir.ActivationFunctionType
AX = mybir.AxisListType

B, S, H, HF = 4, 2048, 2048, 5632
NCH = H // 128
HC = HF // 128
NTOK = 1024
TT = 512
NT = NTOK // TT
WSL = (H * HF) // (8 * 128)
NW = H * HF
EPS_Q = 1e-8
EPS_LN = 1e-5

_TAPS = ()
_CUT = None
_QWIN = False

SWAP32 = [i ^ 1 for i in range(32)]
ENGS = ("sync", "scalar", "tensor", "vector", "gpsimd")


def _host_tables():
    def _mk(xp):
        f32 = xp.float32
        d = H
        half = d // 2
        scale_vec = (xp.arange(0, d, 2, dtype=f32) + f32(0.4 * d)) / f32(1.4 * d)
        power = xp.arange(S, dtype=f32) / f32(512.0)
        scale = scale_vec[None, :] ** power[:, None]
        inv_freq = f32(1.0) / (f32(10000.0) ** (xp.arange(half, dtype=f32) / f32(half)))
        sinusoid = xp.arange(S, dtype=f32)[:, None] * inv_freq[None, :]
        sin = xp.sin(sinusoid)
        cos = xp.cos(sinusoid)
        scale_dn = f32(1.0) / scale

        def rep2(a):
            return xp.repeat(a, 2, axis=-1)
        sin_q = rep2(sin * scale)
        cos_q = rep2(cos * scale)
        sin_k = rep2(sin * scale_dn)
        cos_k = rep2(cos * scale_dn)
        n = xp.arange(S)[:, None]
        m = xp.arange(S)[None, :]
        dm = xp.maximum(n - m, 0).astype(f32)
        D = xp.where(n >= m, f32(0.96875) ** dm, f32(0.0))
        return sin_q, cos_q, sin_k, cos_k, D

    try:
        import jax
        import jax.numpy as jnp
        cpu = jax.devices("cpu")[0]
        with jax.default_device(cpu):
            outs = tuple(np.asarray(o) for o in _mk(jnp))
    except Exception:
        outs = tuple(np.asarray(o) for o in _mk(np))
    sin_q, cos_q, sin_k, cos_k, D = outs
    sgn = np.ones((H,), np.float32)
    sgn[0::2] = -1.0
    sin_q = sin_q * sgn[None, :]
    sin_k = sin_k * sgn[None, :]
    return (np.ascontiguousarray(cos_q.T), np.ascontiguousarray(sin_q.T),
            np.ascontiguousarray(cos_k.T), np.ascontiguousarray(sin_k.T), D)


class Plan:

    def __init__(self, nc):
        self.nc = nc
        self.cnt = defaultdict(int)
        self.ops = {e: [] for e in ENGS}
        self.sems = {}

    def sem(self, name):
        if name not in self.sems:
            self.sems[name] = self.nc.alloc_semaphore(name=name)
        return self.sems[name]

    def emit(self, eng, fn, waits=(), inc=None, incn=1):
        ws = [w for w in waits if w is not None and w[1] and w[1] > 0]
        incs = []
        tok = None
        if inc is not None:
            self.cnt[inc] += incn
            tok = self.cnt[inc]
            incs.append((inc, incn))
        self.ops[eng].append((ws, fn, incs))
        return tok

    def tok(self, sem):
        return self.cnt[sem]

    def realize(self, eng_name, eng):
        lastw = {}
        for ws, fn, incs in self.ops[eng_name]:
            for s, v in ws:
                if lastw.get(s, -1) >= v:
                    continue
                lastw[s] = v
                eng.wait_ge(self.sem(s), v)
            inst = fn(eng)
            for s, n in incs:
                inst.then_inc(self.sem(s), n)


def _build_nc(taps=(), cut=None, qwin=False):
    nc = bass.Bass("TRN2", target_bir_lowering=False, detect_race_conditions=False)

    d_xT = nc.dram_tensor("xT", [H, NTOK], F32, kind="ExternalInput").ap()
    d_wsl = [nc.dram_tensor(f"w{i}sl", [128, WSL], F32, kind="ExternalInput").ap()
             for i in range(3)]
    d_WQ = nc.dram_tensor("WQ", [H, H], F32, kind="ExternalInput").ap()
    d_WK = nc.dram_tensor("WK", [H, H], F32, kind="ExternalInput").ap()
    d_WV = nc.dram_tensor("WV", [2 * H, H // 2], F32, kind="ExternalInput").ap()
    d_cq = nc.dram_tensor("cq", [2 * H, TT], F32, kind="ExternalInput").ap()
    d_sq = nc.dram_tensor("sq", [2 * H, TT], F32, kind="ExternalInput").ap()
    d_ck = nc.dram_tensor("ck", [2 * H, TT], F32, kind="ExternalInput").ap()
    d_sk = nc.dram_tensor("sk", [2 * H, TT], F32, kind="ExternalInput").ap()
    d_dt = nc.dram_tensor("dtt", [2, 8, 128, NTOK], F32, kind="ExternalInput").ap()
    d_id = nc.dram_tensor("ident", [128, 128], F32, kind="ExternalInput").ap()
    d_y = nc.dram_tensor("y", [H, NTOK], F32, kind="ExternalOutput").ap()

    d_gin = nc.dram_tensor("g_in", [1, 4], F32)
    d_gout = nc.dram_tensor("g_out", [1, 4], F32)
    d_bin = nc.dram_tensor("b_in", [1, 4], F32)
    d_bout = nc.dram_tensor("b_out", [1, 4], F32)
    d_bin2 = nc.dram_tensor("b_in2", [1, 4], F32)
    d_bout2 = nc.dram_tensor("b_out2", [1, 4], F32)
    d_qin = [nc.dram_tensor(f"q{i}in", [128, WSL], BF16) for i in range(3)]
    d_qw = [nc.dram_tensor("q0w", [H, HF], BF16),
            nc.dram_tensor("q1w", [H, HF], BF16),
            nc.dram_tensor("q2w", [HF, H], BF16)]
    d_qwb = [nc.dram_tensor("q0wb", [H, HF], BF16),
             nc.dram_tensor("q1wb", [H, HF], BF16),
             nc.dram_tensor("q2wb", [HF, H], BF16)]
    d_qwc = d_qwb
    if qwin:
        d_qwc = [nc.dram_tensor("q0win", [H, HF], BF16, kind="ExternalInput"),
                 nc.dram_tensor("q1win", [H, HF], BF16, kind="ExternalInput"),
                 nc.dram_tensor("q2win", [HF, H], BF16, kind="ExternalInput")]
    d_kvs = nc.dram_tensor("kv_s", [128, 32768], BF16)
    d_kvag = nc.dram_tensor("kv_ag", [1024, 32768], BF16)
    d_kvrb = nc.dram_tensor("kv_rb", [256, 32768], BF16)
    d_kvsel = nc.dram_tensor("kv_sel", [1, 1], mybir.dt.int32, kind="ExternalInput").ap()

    tap_ap = {}
    for t in list(taps) + (["thr_gr", "thr_qred"] if "thr" in taps else []):
        shp, dt = {"xln": ([128, NCH * TT], BF16), "qf": ([128, NCH * NTOK], BF16),
                   "x1": ([128, NCH * NTOK], F32), "x2": ([128, NCH * NTOK], F32),
                   "att": ([128, 9216], BF16), "kvs": ([128, 32768], BF16),
                   "hid": ([128, HC * TT], BF16), "qw0": ([H, HF], BF16), "kvrb": ([256, 32768], BF16),
                   "thr": ([128, 8], F32), "thr_gr": ([1, 16], F32), "lnmr": ([128, 1024], F32), "qw0s": ([128, 1024], BF16), "k1dbg": ([128, 3072], BF16),
                   "thr_qred": ([128, 40], F32)}[t]
        tap_ap[t] = nc.dram_tensor(f"tap_{t}", shp, dt, kind="ExternalOutput").ap()

    A = nc.alloc_sbuf_tensor
    s_xf = A("xf", [128, NCH * NTOK], F32)
    s_xln = A("xlnb", [128, NCH * TT], BF16)
    s_U = A("U", [128, 32768], BF16)
    s_att = A("attsb", [128, 9216], BF16)
    s_wr = A("wring", [128, 8 * TT], BF16)
    s_S1 = A("S1", [128, 4096], BF16)
    s_S2 = A("S2", [128, 3072], F32)
    s_id = A("idsb", [128, 128], F32)
    s_on = A("onescol", [128, 4], F32)
    s_onb = A("onescolb", [128, 1], BF16)
    s_orow = A("onesrow", [1, 128], F32)
    s_thr = A("thr", [128, 8], F32)
    s_gr = A("grow", [1, 16], F32)
    s_kvix = A("kvix", [1, 1], mybir.dt.int32)

    U32 = s_U.ap().bitcast(F32)
    v_qbuf = [U32[:, 0:1408], U32[:, 1408:2816]]
    v_qout = s_U[:, 8192:8192 + WSL]
    S2b = s_S2.ap().bitcast(BF16)
    v_kev = [S2b[:, 0:512], S2b[:, 512:1024]]
    v_kev2 = [S2b[:, 1024:1536], S2b[:, 1536:2048]]
    v_vev = [S2b[:, 2048:3072], S2b[:, 3072:4096]]
    v_kout = [S2b[:, 4096:4608], S2b[:, 4608:5120]]
    v_mrbc = s_S2[:, 0:1024]
    v_m = s_S2[0:1, 1024:1536]
    v_q2 = s_S2[0:1, 1536:2048]
    v_ac = s_S2[0:1, 2048:3072]
    v_sg = s_S1[:, 0:2048]
    v_xbsc = s_S1[:, 2048:4096]
    v_xscr = v_xbsc.bitcast(F32)
    v_qredv = s_S2[:, 0:24]
    s_S3 = s_S2[:, 0:2048]
    s_db = s_U.ap().bitcast(F32)[:, 8192:10240]
    s_vb = s_U[:, 20480:24576]
    s_kb = s_U[:, 24576:28672]
    v_qredw = s_S2[:, 32:35]
    v_qn = s_S1[:, 0:1408]
    v_wqb = [s_S1[:, 0:2048], s_S1[:, 2048:4096]]
    v_wqf = [U32[:, 8192:10240], U32[:, 10240:12288]]
    attF = s_att.ap().bitcast(F32)
    v_tbf = [attF[:, 0:1024], attF[:, 1024:2048]]

    PS = nc.alloc_psum_tensor("PS", [128, 4096], F32)

    def bank(i, n0=0, w=512):
        return PS[:, i * 512 + n0:i * 512 + n0 + w]

    P = Plan(nc)
    E = P.emit
    for s in ("ls", "lg", "sa", "cc", "mm", "vv", "aa"):
        P.sem(s)

    bankfree = {i: None for i in range(8)}

    E("vector", lambda v: v.memset(s_on[:, :], 1.0))
    E("vector", lambda v: v.memset(s_onb[:, :], 1.0))
    E("vector", lambda v: v.memset(s_orow[:, :], 1.0))
    E("vector", lambda v: v.memset(s_gr[:, :], 0.0), inc="vv")
    E("gpsimd", lambda g: g.dma_start(out=s_id[:, :], in_=d_id[:, :]), inc="lg", incn=16)
    for dc in range(NCH):
        E("gpsimd", lambda g, dc=dc: g.dma_start(
            out=s_xf[:, dc * NTOK:(dc + 1) * NTOK],
            in_=d_xT[dc * 128:(dc + 1) * 128, :]), inc="lg", incn=16)
    t_xload = P.tok("lg")

    NQT = 8
    QW = WSL // NQT
    red_tok = []
    for wi in range(3):
        for k in range(NQT):
            i = wi * NQT + k
            sl = i % 2
            wprev = ("vv", red_tok[i - 2]) if i >= 2 else None
            E("sync", lambda s, wi=wi, k=k, sl=sl: s.dma_start(
                out=v_qbuf[sl], in_=d_wsl[wi][:, k * QW:(k + 1) * QW]),
                waits=[wprev], inc="ls", incn=16)
            tl = P.tok("ls")
            E("vector", lambda v, i=i, sl=sl: v.tensor_reduce(
                v_qredv[:, i:i + 1], v_qbuf[sl], AX.X, OP.add, True),
                waits=[("ls", tl)], inc="vv")
            red_tok.append(P.tok("vv"))
    for wi in range(3):
        E("vector", lambda v, wi=wi: v.tensor_reduce(
            v_qredw[:, wi:wi + 1], v_qredv[:, wi * NQT:(wi + 1) * NQT],
            AX.X, OP.add, False), inc="vv")
    t_qred = P.tok("vv")
    E("tensor", lambda t: t.matmul(PS[0:1, 3072:3075], s_on[:, 0:1], v_qredw[:, 0:3],
                                   start=True, stop=True),
      waits=[("vv", t_qred)], inc="mm")
    t_gmm = P.tok("mm")
    E("vector", lambda v: v.tensor_copy(s_gr[0:1, 0:3], PS[0:1, 3072:3075]),
      waits=[("mm", t_gmm)], inc="vv")
    t_gcp = P.tok("vv")
    E("gpsimd", lambda g: g.dma_start(out=d_gin.ap()[0:1, 0:4], in_=s_gr[0:1, 0:4]),
      waits=[("vv", t_gcp)], inc="lg", incn=16)
    t_gw = P.tok("lg")
    E("gpsimd", lambda g: g.collective_compute(
        "AllReduce", OP.add, replica_groups=[list(range(8))],
        ins=[d_gin.ap().opt()], outs=[d_gout.ap().opt()]),
        waits=[("lg", t_gw)], inc="cc")
    t_ar = P.tok("cc")
    E("gpsimd", lambda g: g.dma_start(out=s_gr[0:1, 4:8], in_=d_gout.ap()[0:1, 0:4]),
      waits=[("cc", t_ar)], inc="lg", incn=16)
    t_grl = P.tok("lg")
    c1 = float(np.float32(0.5) / np.float32(NW))
    c2 = float(0.5 * EPS_Q)
    E("tensor", lambda t: t.matmul(PS[:, 3072:3075], s_orow[0:1, :], s_gr[0:1, 4:7],
                                   start=True, stop=True),
      waits=[("lg", t_grl)], inc="mm")
    t_tbc = P.tok("mm")
    E("vector", lambda v: v.tensor_scalar(s_thr[:, 0:3], PS[:, 3072:3075], c1, c2,
                                          OP.mult, OP.add),
      waits=[("mm", t_tbc)])
    E("vector", lambda v: v.tensor_scalar(s_thr[:, 3:6], PS[:, 3072:3075], -c1, -c2,
                                          OP.mult, OP.add), inc="vv")
    t_thr = P.tok("vv")
    bankfree[6] = bankfree[7] = ("vv", t_thr)

    t_qag = []
    qv_tok = []
    qw_tok = {}
    for wi in range(3):
        t_last = 0
        for k in range(NQT):
            i = wi * NQT + k
            sl = i % 2
            wprev = ("vv", qv_tok[i - 2]) if i >= 2 else ("vv", t_thr)
            E("sync", lambda s, wi=wi, k=k, sl=sl: s.dma_start(
                out=v_qbuf[sl], in_=d_wsl[wi][:, k * QW:(k + 1) * QW]),
                waits=[wprev, ("cc", t_ar)], inc="ls", incn=16)
            tl = P.tok("ls")
            qp = v_qout[:, k * QW:(k + 1) * QW]
            wq = ("lg", qw_tok[(wi - 1, k)]) if wi > 0 else None
            E("vector", lambda v, qp=qp, sl=sl, wi=wi: v.tensor_scalar(
                qp, v_qbuf[sl], s_thr[:, wi:wi + 1], None, OP.is_gt),
                waits=[("ls", tl), ("vv", t_thr), wq])
            E("vector", lambda v, sl=sl, wi=wi: v.tensor_scalar(
                v_qn, v_qbuf[sl], s_thr[:, 3 + wi:4 + wi], None, OP.is_lt))
            E("vector", lambda v, qp=qp: v.tensor_sub(qp, qp, v_qn), inc="vv")
            qv_tok.append(P.tok("vv"))
            E("gpsimd", lambda g, wi=wi, k=k, qp=qp: g.dma_start(
                out=d_qin[wi][:, k * QW:(k + 1) * QW], in_=qp),
                waits=[("vv", qv_tok[-1])], inc="lg", incn=16)
            t_last = P.tok("lg")
            qw_tok[(wi, k)] = t_last
    for wi in range(3):
        E("gpsimd", lambda g, wi=wi: g.collective_compute(
            "AllGather", OP.bypass, replica_groups=[list(range(8))],
            ins=[d_qin[wi].ap().opt()], outs=[d_qw[wi].ap().opt()]),
            waits=[("lg", t_last)], inc="cc")
    E("gpsimd", lambda g: g.collective_compute(
        "AllReduce", OP.add, replica_groups=[list(range(8))],
        ins=[d_bin.ap().opt()], outs=[d_bout.ap().opt()]), inc="cc")
    t_bar = P.tok("cc")
    snapA = [[None] * NCH, [None] * NCH]
    snapB = [None] * HC
    for dc in range(NCH):
        for wi in range(2):
            E("gpsimd", lambda g, wi=wi, dc=dc: g.dma_start(
                out=d_qwb[wi].ap()[dc * 128:(dc + 1) * 128, :],
                in_=d_qw[wi].ap()[dc * 128:(dc + 1) * 128, :]),
                waits=[("cc", t_bar)], inc="lg", incn=16)
            snapA[wi][dc] = ("lg", P.tok("lg"))
    for hc in range(HC):
        E("gpsimd", lambda g, hc=hc: g.dma_start(
            out=d_qwb[2].ap()[hc * 128:(hc + 1) * 128, :],
            in_=d_qw[2].ap()[hc * 128:(hc + 1) * 128, :]),
            waits=[("cc", t_bar)], inc="lg", incn=16)
        snapB[hc] = ("lg", P.tok("lg"))
    t_qag.extend([snapA[0][NCH - 1], snapA[1][NCH - 1], snapB[HC - 1]])
    if "thr" in tap_ap:
        E("gpsimd", lambda g: g.dma_start(out=tap_ap["thr"], in_=s_thr[:, :]),
          waits=[("vv", t_thr)], inc="lg", incn=16)
        E("gpsimd", lambda g: g.dma_start(out=tap_ap["thr_gr"], in_=s_gr[:, :]),
          waits=[("vv", t_thr)], inc="lg", incn=16)
        E("gpsimd", lambda g: g.dma_start(out=tap_ap["thr_qred"][:, 0:35], in_=s_S2[:, 0:35]),
          waits=[("vv", t_thr)], inc="lg", incn=16)
    if "qw0" in tap_ap:
        E("sync", lambda s: s.dma_start(out=tap_ap["qw0"], in_=d_qw[0].ap()),
          waits=[t_qag[2]], inc="ls", incn=16)


    lnslot = [0, 0]

    def ln_tile(tt, extra_waits=()):
        fence = [("mm", P.tok("mm")), ("lg", P.tok("lg")), ("aa", P.tok("aa"))]
        for dc in range(NCH):
            sl = dc % 2
            xsl = s_xf[:, dc * NTOK + tt * TT: dc * NTOK + tt * TT + TT]
            xb = v_xbsc[:, sl * 512:sl * 512 + 512]
            xq = v_xbsc[:, 1024 + sl * 512:1024 + sl * 512 + 512]
            ws = (list(extra_waits) + fence) if dc == 0 else [("mm", lnslot[sl])]
            E("vector", lambda v, xb=xb, xsl=xsl: v.tensor_copy(xb, xsl), waits=ws)
            E("vector", lambda v, xb=xb, xq=xq: v.tensor_mul(xq, xb, xb), inc="vv")
            tv = P.tok("vv")
            ws2 = [("vv", tv)]
            if dc == 0:
                ws2 += [bankfree[6], bankfree[7]]
            E("tensor", lambda t, xb=xb, dc=dc: t.matmul(
                PS[0:1, 3072:3584], s_onb[:, 0:1], xb,
                start=(dc == 0), stop=(dc == NCH - 1)), waits=ws2)
            E("tensor", lambda t, xq=xq, dc=dc: t.matmul(
                PS[0:1, 3584:4096], s_onb[:, 0:1], xq,
                start=(dc == 0), stop=(dc == NCH - 1)), inc="mm")
            lnslot[sl] = P.tok("mm")
        t_st = P.tok("mm")
        cN = float(np.float32(1.0) / np.float32(H))
        E("vector", lambda v: v.tensor_copy(v_m, PS[0:1, 3072:3584]),
          waits=[("mm", t_st)])
        E("vector", lambda v: v.tensor_copy(v_q2, PS[0:1, 3584:4096]), inc="vv")
        tv = P.tok("vv")
        E("tensor", lambda t: t.matmul(PS[:, 3072:3584], s_orow[0:1, :], v_m,
                                       start=True, stop=True), waits=[("vv", tv)])
        E("tensor", lambda t: t.matmul(PS[:, 3584:4096], s_orow[0:1, :], v_q2,
                                       start=True, stop=True), inc="mm")
        tm = P.tok("mm")
        sA = v_xscr[:, 0:512]
        sB = v_xscr[:, 512:1024]
        E("vector", lambda v: v.tensor_scalar(v_mrbc[:, 512:1024], PS[:, 3072:3584],
                                              cN, None, OP.mult), waits=[("mm", tm)])
        E("vector", lambda v: v.tensor_scalar(sA, PS[:, 3584:4096], cN, None, OP.mult),
          inc="vv")
        t1_ = P.tok("vv")
        bankfree[6] = bankfree[7] = ("vv", t1_)
        E("vector", lambda v: v.tensor_mul(sB, v_mrbc[:, 512:1024],
                                           v_mrbc[:, 512:1024]),
          waits=[("vv", t1_)], inc="vv")
        t2_ = P.tok("vv")
        E("vector", lambda v: v.tensor_sub(sA, sA, sB), waits=[("vv", t2_)], inc="vv")
        t3_ = P.tok("vv")
        E("vector", lambda v: v.tensor_scalar(sA, sA, EPS_LN, None, OP.add),
          waits=[("vv", t3_)], inc="vv")
        t4_ = P.tok("vv")
        E("scalar", lambda a: a.activation(sB, sA, AF.Ln), waits=[("vv", t4_)],
          inc="aa")
        t5_ = P.tok("aa")
        E("scalar", lambda a: a.activation(v_mrbc[:, 0:512], sB, AF.Exp, scale=-0.5),
          waits=[("aa", t5_)], inc="aa")
        t_exp = P.tok("aa")
        for dc in range(NCH):
            sl = dc % 2
            xsl = s_xf[:, dc * NTOK + tt * TT: dc * NTOK + tt * TT + TT]
            u = v_xscr[:, sl * 512:sl * 512 + 512]
            ws3 = [("aa", t_exp)] if dc == 0 else []
            E("vector", lambda v, u=u, xsl=xsl: v.tensor_tensor(
                u, xsl, v_mrbc[:, 512:1024], OP.subtract), waits=ws3)
            E("vector", lambda v, u=u, dc=dc: v.tensor_tensor(
                s_xln[:, dc * TT:(dc + 1) * TT], u, v_mrbc[:, 0:512], OP.mult),
                inc="vv")
        return P.tok("vv")

    wr_n = [0]
    wr_free = [None] * 8

    def wr_load(src_fn, extra=None):
        slot = wr_n[0] % 8
        E("sync", lambda s, slot=slot: s.dma_start(
            out=s_wr[:, slot * TT:(slot + 1) * TT], in_=src_fn()),
            waits=[wr_free[slot], extra], inc="ls", incn=16)
        wr_n[0] += 1
        return slot, P.tok("ls")

    def ffn(app, tt, xln_tok):
        for g in range(11):
            stop_tok = {}
            for which in (0, 1):
                base = 0 if which == 0 else 4
                for dc in range(NCH):
                    slot, tl = wr_load(lambda wi=which, g=g, dc=dc: d_qwc[wi].ap()[
                        dc * 128:(dc + 1) * 128, g * 512:(g + 1) * 512],
                        extra=snapA[which][dc])
                    tok = None
                    for k in range(4):
                        bk = base + k
                        ws = []
                        if k == 0:
                            ws.append(("ls", tl))
                        if dc == 0:
                            ws.append(bankfree[bk])
                            if k == 0:
                                ws.append(("vv", xln_tok))
                        tok = E("tensor", lambda t, slot=slot, k=k, dc=dc, bk=bk:
                                t.matmul(
                            bank(bk),
                            s_wr[:, slot * TT + k * 128: slot * TT + (k + 1) * 128],
                            s_xln[:, dc * TT:(dc + 1) * TT],
                            start=(dc == 0), stop=(dc == NCH - 1)),
                            waits=ws, inc="mm")
                        if dc == NCH - 1:
                            stop_tok[bk] = tok
                    wr_free[slot] = ("mm", tok)
            for k in range(4):
                E("scalar", lambda a, k=k: a.activation(
                    v_sg[:, k * 512:(k + 1) * 512], bank(k), AF.Sigmoid),
                    waits=[("mm", stop_tok[k])], inc="aa")
                bankfree[k] = ("aa", P.tok("aa"))
            ta = P.tok("aa")
            for k in range(4):
                hc = 4 * g + k
                E("vector", lambda v, k=k, hc=hc: v.tensor_tensor(
                    s_U[:, hc * TT:(hc + 1) * TT], v_sg[:, k * 512:(k + 1) * 512],
                    bank(4 + k), OP.mult),
                    waits=[("aa", ta), ("mm", stop_tok[4 + k])], inc="vv")
                bankfree[4 + k] = ("vv", P.tok("vv"))
        t_hid = P.tok("vv")
        if "hid" in tap_ap and app == 0 and tt == 0:
            E("gpsimd", lambda g: g.dma_start(out=tap_ap["hid"], in_=s_U[:, 0:HC * TT]),
              waits=[("vv", t_hid)], inc="lg", incn=16)
        for dg in range(4):
            stop_tok = {}
            for hc in range(HC):
                slot, tl = wr_load(lambda dg=dg, hc=hc: d_qwc[2].ap()[
                    hc * 128:(hc + 1) * 128, dg * 512:(dg + 1) * 512],
                    extra=snapB[hc])
                tok = None
                for k in range(4):
                    ws = []
                    if k == 0:
                        ws.append(("ls", tl))
                    if hc == 0:
                        ws.append(bankfree[k])
                        if k == 0 and dg == 0:
                            ws.append(("vv", t_hid))
                    tok = E("tensor", lambda t, slot=slot, k=k, hc=hc: t.matmul(
                        bank(k),
                        s_wr[:, slot * TT + k * 128: slot * TT + (k + 1) * 128],
                        s_U[:, hc * TT:(hc + 1) * TT],
                        start=(hc == 0), stop=(hc == HC - 1)),
                        waits=ws, inc="mm")
                    if hc == HC - 1:
                        stop_tok[k] = tok
                wr_free[slot] = ("mm", tok)
            for k in range(4):
                dc = 4 * dg + k
                xsl = s_xf[:, dc * NTOK + tt * TT: dc * NTOK + tt * TT + TT]
                E("vector", lambda v, k=k, xsl=xsl: v.scalar_tensor_tensor(
                    xsl, bank(k), 0.5, xsl, OP.mult, OP.add),
                    waits=[("mm", stop_tok[k])], inc="vv")
                bankfree[k] = ("vv", P.tok("vv"))
                if app == 1:
                    tv = P.tok("vv")
                    E("scalar", lambda a, dc=dc, xsl=xsl: a.dma_start(
                        out=d_y[dc * 128:(dc + 1) * 128, tt * TT:(tt + 1) * TT],
                        in_=xsl), waits=[("vv", tv)], inc="sa", incn=16)
        return P.tok("vv")

    def _finish_early():
        tv = P.tok("vv")
        ta = P.tok("aa")
        tm = P.tok("mm")
        for dc in range(NCH):
            E("scalar", lambda a, dc=dc: a.dma_start(
                out=d_y[dc * 128:(dc + 1) * 128, :],
                in_=s_xf[:, dc * NTOK:(dc + 1) * NTOK]),
                waits=[("vv", tv), ("aa", ta), ("mm", tm), ("lg", P.tok("lg")),
                       ("ls", P.tok("ls")), ("cc", P.tok("cc"))],
                inc="sa", incn=16)
        E("scalar", lambda a: a.nop(), waits=[("sa", P.tok("sa"))])

    def _realize():
        with nc.Block() as block:
            @block.sync
            def _(e):
                P.realize("sync", e)

            @block.scalar
            def _(e):
                P.realize("scalar", e)

            @block.tensor
            def _(e):
                P.realize("tensor", e)

            @block.vector
            def _(e):
                P.realize("vector", e)

            @block.gpsimd
            def _(e):
                P.realize("gpsimd", e)

    if cut == "quant":
        _finish_early(); _realize(); return nc

    tok = ln_tile(0, extra_waits=[("lg", t_xload)])
    tok_f0 = ffn(0, 0, tok)
    tok = ln_tile(1)
    if "lnmr" in tap_ap:
        E("gpsimd", lambda g: g.dma_start(out=tap_ap["lnmr"], in_=v_mrbc[:, 0:1024]),
          waits=[("vv", tok)], inc="lg", incn=16)
    if "xln" in tap_ap and cut == "ffn1":
        E("gpsimd", lambda g: g.dma_start(out=tap_ap["xln"], in_=s_xln[:, :]),
          waits=[("vv", tok)], inc="lg", incn=16)
    tok_f1 = ffn(0, 1, tok)
    t_ffn1 = max(tok_f0, tok_f1)

    if "x1" in tap_ap:
        E("gpsimd", lambda g: g.dma_start(out=tap_ap["x1"], in_=s_xf[:, :]),
          waits=[("vv", t_ffn1)], inc="lg", incn=16)

    if cut == "ffn1":
        _finish_early(); _realize(); return nc

    wq_free = [None, None]
    tb_free = [None, None]
    kout_free = [None, None]
    vev_free = [None, None]
    kvg_wb = [None, None]
    pbank = [0]
    proj_last = [0]

    def _kvs_k_dst(jc, tt):
        base = tt * 4 * 2048 + jc * 128
        ap = d_kvs.ap()
        flat = ap[:, base:base + 3 * 2048 + 128]
        return bass.AP(flat.tensor, flat.offset,
                       [list(flat.ap[0]), [2048, 4], [1, 128]])

    def proj_qk(which, tt, xln_tok):
        dW = d_WQ if which == "q" else d_WK
        dcos = d_cq if which == "q" else d_ck
        dsin = d_sq if which == "q" else d_sk
        def _wload(jc):
            E("gpsimd", lambda g, jc=jc: g.dma_start(
                out=v_wqf[jc % 2], in_=dW[jc * 128:(jc + 1) * 128, :]),
                waits=[wq_free[jc % 2], ("vv", t_ffn1)], inc="lg", incn=16)
            return P.tok("lg")
        _wload(0)
        for jc in range(NCH):
            sl = jc % 2
            wsl_ = v_wqb[sl]
            wfs = v_wqf[sl]
            tlwf = _wload(jc + 1) if jc + 1 < NCH else P.tok("lg")
            E("vector", lambda v, wsl_=wsl_, wfs=wfs: v.scalar_tensor_tensor(
                wsl_, wfs, 1.0, wfs, OP.mult, OP.max),
                waits=[("lg", tlwf), wq_free[sl]], inc="vv")
            tlw = ("vv", P.tok("vv"))
            ctb = v_tbf[sl][:, 0:512]
            stb = v_tbf[sl][:, 512:1024]
            blk = jc * 2 + tt
            E("scalar", lambda a, ctb=ctb, dcos=dcos, blk=blk: a.dma_start(
                out=ctb, in_=dcos[blk * 128:(blk + 1) * 128, :]),
                waits=[tb_free[sl]], inc="sa", incn=16)
            E("scalar", lambda a, stb=stb, dsin=dsin, blk=blk: a.dma_start(
                out=stb, in_=dsin[blk * 128:(blk + 1) * 128, :]),
                inc="sa", incn=16)
            tlt = P.tok("sa")
            bk = pbank[0] % 4
            pbank[0] += 1
            ws0 = [tlw, bankfree[bk]]
            if jc == 0:
                ws0.append(("vv", xln_tok))
            for dc in range(NCH):
                E("tensor", lambda t, wsl_=wsl_, dc=dc, bk=bk: t.matmul(
                    bank(bk), wsl_[:, dc * 128:(dc + 1) * 128],
                    s_xln[:, dc * TT:(dc + 1) * TT],
                    start=(dc == 0), stop=(dc == NCH - 1)),
                    waits=(ws0 if dc == 0 else []), inc="mm")
            tmm = P.tok("mm")
            wq_free[sl] = ("mm", tmm)
            kb, rot = v_kev[0], v_kev[1]
            tb_, ub = v_kev2[0], v_kev2[1]
            if which == "q":
                out = s_U[:, jc * NTOK + tt * TT: jc * NTOK + tt * TT + TT]
            else:
                out = v_kout[1 - sl]
            ws1 = [("mm", tmm), ("sa", tlt), ("vv", proj_last[0])]
            if which == "k":
                ws1.append(kout_free[1 - sl])
            E("vector", lambda v, kb=kb, bk=bk: v.tensor_copy(kb, bank(bk)),
              waits=ws1, inc="vv")
            e1 = P.tok("vv")
            E("vector", lambda v, rot=rot, kb=kb: v.stream_shuffle(rot, kb, SWAP32),
              waits=[("vv", e1)], inc="vv")
            e2 = P.tok("vv")
            E("vector", lambda v, tb_=tb_, rot=rot, stb=stb: v.tensor_mul(tb_, rot, stb),
              waits=[("vv", e2)], inc="vv")
            e3 = P.tok("vv")
            E("vector", lambda v, ub=ub, kb=kb, ctb=ctb: v.tensor_mul(ub, kb, ctb),
              waits=[("vv", e3)], inc="vv")
            e4 = P.tok("vv")
            bankfree[bk] = ("vv", e4)
            tb_free[sl] = ("vv", e4)
            if which == "q":
                E("vector", lambda v, out=out, ub=ub, tb_=tb_: v.tensor_add(out, ub, tb_),
                  waits=[("vv", e4)], inc="vv")
            else:
                E("vector", lambda v, out=out, ub=ub, tb_=tb_: v.tensor_add(out, ub, tb_),
                  waits=[("vv", e4)], inc="vv")
            proj_last[0] = P.tok("vv")
            if which == "k":
                if "k1dbg" in tap_ap and jc == 1 and tt == 0:
                    E("gpsimd", lambda g: g.dma_start(
                        out=tap_ap["k1dbg"][:, 2048:3072],
                        in_=s_att[:, 2048:3072]),
                        waits=[("vv", proj_last[0])], inc="lg", incn=16)
                if "k1dbg" in tap_ap and jc in (1, 3):
                    E("gpsimd", lambda g, jc=jc, tt=tt, sl=sl: g.dma_start(
                        out=tap_ap["k1dbg"][:, (jc // 2) * 1024 + tt * TT:
                                            (jc // 2) * 1024 + tt * TT + TT],
                        in_=v_kout[1 - sl]),
                        waits=[("vv", proj_last[0])], inc="lg", incn=16)
                E("gpsimd", lambda g, jc=jc, tt=tt, sl=sl: g.dma_start(
                    out=d_kvs.ap()[:, jc * 1024 + tt * TT: jc * 1024 + tt * TT + TT],
                    in_=v_kout[1 - sl]),
                    waits=[("vv", proj_last[0])], inc="lg", incn=16)
                kout_free[1 - sl] = ("lg", P.tok("lg"))
        return proj_last[0]

    def proj_v(tt, xln_tok):
        def _vload(vblk):
            E("gpsimd", lambda g, vblk=vblk: g.dma_start(
                out=v_wqf[vblk % 2][:, 0:1024],
                in_=d_WV[vblk * 128:(vblk + 1) * 128, :]),
                waits=[wq_free[vblk % 2], ("vv", t_ffn1)], inc="lg", incn=16)
            return P.tok("lg")
        _vload(0)
        for half in range(2):
            stop_tok = {}
            for dc in range(NCH):
                sl = dc % 2
                wsl_ = v_wqb[sl]
                wfs = v_wqf[sl]
                vblk = half * NCH + dc
                tlwf = _vload(vblk + 1) if vblk + 1 < 2 * NCH else P.tok("lg")
                E("vector", lambda v, wsl_=wsl_, wfs=wfs: v.scalar_tensor_tensor(
                    wsl_[:, 0:1024], wfs[:, 0:1024], 1.0, wfs[:, 0:1024],
                    OP.mult, OP.max),
                    waits=[("lg", tlwf), wq_free[sl]], inc="vv")
                tlw = ("vv", P.tok("vv"))
                tok = None
                for tb in range(4):
                    for n in range(2):
                        bk = 2 * tb + n
                        ws = []
                        if tb == 0 and n == 0:
                            ws.append(tlw)
                        if dc == 0:
                            ws.append(bankfree[bk])
                            if tb == 0 and n == 0:
                                ws.append(("vv", xln_tok))
                        tok = E("tensor",
                                lambda t, wsl_=wsl_, dc=dc, tb=tb, n=n, bk=bk: t.matmul(
                            bank(bk),
                            s_xln[:, dc * TT + tb * 128: dc * TT + (tb + 1) * 128],
                            wsl_[:, n * 512:(n + 1) * 512],
                            start=(dc == 0), stop=(dc == NCH - 1)),
                            waits=ws, inc="mm")
                        if dc == NCH - 1:
                            stop_tok[bk] = tok
                wq_free[sl] = ("mm", tok)
            for tb in range(4):
                sl = tb % 2
                E("vector", lambda v, tb=tb, sl=sl: v.tensor_copy(
                    v_vev[sl], PS[:, tb * 1024:(tb + 1) * 1024]),
                    waits=[("mm", stop_tok[2 * tb + 1]), vev_free[sl]], inc="vv")
                tv0 = P.tok("vv")
                E("vector", lambda v, sl=sl: v.tensor_tensor(
                    v_vev[sl], v_vev[sl], v_vev[sl], OP.max),
                    waits=[("vv", tv0)], inc="vv")
                tv = P.tok("vv")
                bankfree[2 * tb] = bankfree[2 * tb + 1] = ("vv", tv)
                tbg = tt * 4 + tb
                E("gpsimd", lambda g, tbg=tbg, half=half, sl=sl: g.dma_start(
                    out=d_kvs.ap()[:, 16384 + tbg * 2048 + half * 1024:
                                   16384 + tbg * 2048 + (half + 1) * 1024],
                    in_=v_vev[sl]), waits=[("vv", tv)], inc="lg", incn=16)
                vev_free[sl] = ("lg", P.tok("lg"))
        return P.tok("lg")

    t_ag_trigger = None
    tokl0 = ln_tile(0, extra_waits=[("vv", t_ffn1)])
    proj_qk("k", 0, tokl0)
    proj_v(0, tokl0)
    proj_qk("q", 0, tokl0)
    tokl1 = ln_tile(1)
    proj_qk("k", 1, tokl1)
    t_kv_done = proj_v(1, tokl1)
    t_q_done = proj_qk("q", 1, tokl1)

    E("gpsimd", lambda g: g.dma_start(out=s_kvix[:, :], in_=d_kvsel[:, :]),
      inc="lg", incn=16)
    t_ixl = P.tok("lg")
    E("gpsimd", lambda g: g.collective_compute(
        "AllGather", OP.bypass, replica_groups=[list(range(8))],
        ins=[d_kvs.ap().opt()], outs=[d_kvag.ap().opt()]),
        waits=[("lg", P.tok("lg")), ("ls", P.tok("ls")), ("sa", P.tok("sa")),
               ("vv", t_q_done)], inc="cc")
    for _ in range(3):
        E("gpsimd", lambda g: g.collective_compute(
            "AllReduce", OP.add, replica_groups=[list(range(8))],
            ins=[d_bin2.ap().opt()], outs=[d_bout2.ap().opt()]), inc="cc")
    t_cc2 = P.tok("cc")
    def _kv_snap(g):
        r = g.alloc_register("kvbase")
        g.reg_load(r, s_kvix[0:1, 0:1])
        sv = RuntimeValue(r, min_val=0, max_val=768)
        return g.dma_start(out=d_kvrb.ap()[:, :],
                           in_=d_kvag.ap()[bass.ds(sv, 256), :])
    E("gpsimd", _kv_snap, waits=[("cc", t_cc2), ("lg", t_ixl)], inc="lg", incn=16)
    t_kvag = ("lg", P.tok("lg"))

    if "kvs" in tap_ap:
        E("gpsimd", lambda g: g.dma_start(out=tap_ap["kvs"], in_=d_kvs.ap()),
          waits=[("lg", t_kv_done)], inc="lg", incn=16)
    if "xln" in tap_ap:
        E("gpsimd", lambda g: g.dma_start(out=tap_ap["xln"], in_=s_xln[:, :]),
          waits=[("vv", t_q_done)], inc="lg", incn=16)
    if "qf" in tap_ap:
        E("gpsimd", lambda g: g.dma_start(out=tap_ap["qf"], in_=s_U[:, 0:NCH * NTOK]),
          waits=[("vv", t_q_done)], inc="lg", incn=16)

    if cut == "proj":
        _finish_early(); _realize(); return nc

    def nq(j):
        return NTOK - 128 * j

    cb = {}
    off = 0
    for kh in range(2):
        for j in range(8):
            cb[(kh, j)] = off
            off += nq(j)
    assert off == 9216

    kb_free = [None, None]
    db_free = [None, None]
    vb_free = [None, None]
    att_tok = {}
    abank_n = [0]
    vb_n = [0]
    ab_n = [0]
    tp_free = [None] * 2
    avac_free = [None]

    def _kvrb_k_src(kh, j):
        base = kh * 128 * 32768 + (j // 4) * TT + (j % 4) * 128
        ap = d_kvrb.ap()
        return bass.AP(ap.tensor, base, [[32768, 128], [1024, NCH], [1, 128]])

    def phase_a(kh, j):
        sl = ab_n[0] % 2
        ab_n[0] += 1
        E("sync", lambda s, kh=kh, j=j, sl=sl: s.dma_start(
            out=s_kb[:, sl * 2048:(sl + 1) * 2048].rearrange("p (c u) -> p c u", c=NCH),
            in_=_kvrb_k_src(kh, j)),
            waits=[t_kvag, kb_free[sl]], inc="ls", incn=16)
        tlk = P.tok("ls")
        E("sync", lambda s, kh=kh, j=j, sl=sl: s.dma_start(
            out=s_db[:, sl * 1024:sl * 1024 + nq(j)],
            in_=d_dt[kh:kh + 1, j:j + 1, :, 128 * j:NTOK]),
            waits=[db_free[sl]], inc="ls", incn=16)
        tld = P.tok("ls")
        aslot = abank_n[0] % 2
        abank_n[0] += 1
        b0 = 2 * aslot
        nsg = (nq(j) + 511) // 512
        for sg in range(nsg):
            w = min(512, nq(j) - sg * 512)
            for dc in range(NCH):
                ws2 = []
                if dc == 0:
                    ws2 = [("ls", tlk), bankfree[b0 + sg]]
                    if sg == 0:
                        ws2.append(("vv", t_q_done))
                E("tensor", lambda t, sl=sl, dc=dc, sg=sg, w=w, j=j, b0=b0: t.matmul(
                    bank(b0 + sg, 0, w),
                    s_kb[:, sl * 2048 + dc * 128: sl * 2048 + (dc + 1) * 128],
                    s_U[:, dc * NTOK + 128 * j + sg * 512:
                         dc * NTOK + 128 * j + sg * 512 + w],
                    start=(dc == 0), stop=(dc == NCH - 1)), waits=ws2, inc="mm")
        tmm = P.tok("mm")
        kb_free[sl] = ("mm", tmm)
        E("vector", lambda v, kh=kh, j=j, sl=sl, b0=b0: v.tensor_tensor(
            s_att[:, cb[(kh, j)]:cb[(kh, j)] + nq(j)],
            PS[:, b0 * 512:b0 * 512 + nq(j)],
            s_db[:, sl * 1024:sl * 1024 + nq(j)], OP.mult),
            waits=[("mm", tmm), ("ls", tld)], inc="vv")
        tv = P.tok("vv")
        att_tok[(kh, j)] = tv
        db_free[sl] = ("vv", tv)
        for sg in range(nsg):
            bankfree[b0 + sg] = ("vv", tv)

    def phase_b(i):
        pairs = [(kh, j) for j in range(i + 1) for kh in range(2)]
        stop_tok = {}
        for pi, (kh, j) in enumerate(pairs):
            sl = vb_n[0] % 2
            vb_n[0] += 1
            E("sync", lambda s, kh=kh, j=j, sl=sl: s.dma_start(
                out=s_vb[:, sl * 2048:(sl + 1) * 2048],
                in_=d_kvrb.ap()[kh * 128:(kh + 1) * 128,
                                16384 + j * 2048:16384 + (j + 1) * 2048]),
                waits=[t_kvag, vb_free[sl]], inc="ls", incn=16)
            tlv = P.tok("ls")
            atile = s_att[:, cb[(kh, j)] + 128 * (i - j):
                          cb[(kh, j)] + 128 * (i - j) + 128]
            tok = None
            for n in range(4):
                bk = 4 + n
                ws2 = [("vv", att_tok[(kh, j)])]
                if n == 0:
                    ws2.append(("ls", tlv))
                if pi == 0:
                    ws2.append(bankfree[bk])
                tok = E("tensor", lambda t, atile=atile, sl=sl, n=n, bk=bk,
                        st_=(pi == 0), sp_=(pi == len(pairs) - 1): t.matmul(
                    bank(bk), atile,
                    s_vb[:, sl * 2048 + n * 512: sl * 2048 + (n + 1) * 512],
                    start=st_, stop=sp_),
                    waits=ws2, inc="mm")
                if pi == len(pairs) - 1:
                    stop_tok[bk] = tok
            vb_free[sl] = ("mm", tok)
        E("vector", lambda v: v.tensor_copy(s_S3, PS[:, 2048:4096]),
          waits=[("mm", stop_tok[7]), avac_free[0]], inc="vv")
        tv = P.tok("vv")
        for bk in range(4, 8):
            bankfree[bk] = ("vv", tv)
        aslot = (abank_n[0] + 1) % 2
        b0 = 2 * aslot
        tadd = tv
        for dc in range(NCH):
            bsel = b0 + (dc % 2)
            q = (dc // 2) % 2
            tp = PS[:, bsel * 512 + q * 128: bsel * 512 + (q + 1) * 128]
            ws = [("vv", tv), tp_free[dc % 2]]
            if dc < 2:
                ws += [bankfree[bsel]]
            E("tensor", lambda t, tp=tp, dc=dc: t.transpose(
                tp, s_S3[:, dc * 128:(dc + 1) * 128], s_id[:, :]),
                waits=ws, inc="mm")
            tm = P.tok("mm")
            xsl = s_xf[:, dc * NTOK + i * 128: dc * NTOK + (i + 1) * 128]
            E("vector", lambda v, xsl=xsl, tp=tp: v.tensor_add(xsl, xsl, tp),
              waits=[("mm", tm)], inc="vv")
            tp_free[dc % 2] = ("vv", P.tok("vv"))
            tadd = P.tok("vv")
        bankfree[b0] = bankfree[b0 + 1] = ("vv", tadd)
        avac_free[0] = ("vv", tadd)
        return tadd

    t_attn = 0
    for j in range(8):
        phase_a(0, j)
        phase_a(1, j)
        t_attn = phase_b(j)

    if "kvrb" in tap_ap:
        E("gpsimd", lambda g: g.dma_start(out=tap_ap["kvrb"], in_=d_kvrb.ap()),
          waits=[t_kvag], inc="lg", incn=16)
    if "att" in tap_ap:
        E("gpsimd", lambda g: g.dma_start(out=tap_ap["att"], in_=s_att[:, :]),
          waits=[("vv", t_attn)], inc="lg", incn=16)
    if "x2" in tap_ap:
        E("gpsimd", lambda g: g.dma_start(out=tap_ap["x2"], in_=s_xf[:, :]),
          waits=[("vv", t_attn)], inc="lg", incn=16)

    if cut == "attn":
        _finish_early(); _realize(); return nc

    for tt in range(NT):
        tokl = ln_tile(tt, extra_waits=[("vv", t_attn)])
        ffn(1, tt, tokl)
    n_st = P.tok("sa")
    E("scalar", lambda a: a.nop(), waits=[("sa", n_st)])
    _realize()
    return nc


_NC_CACHE = {}


def _get_nc(taps=()):
    key = (tuple(taps), _CUT, _QWIN)
    if key not in _NC_CACHE:
        _NC_CACHE[key] = _build_nc(taps, _CUT, _QWIN)
    return _NC_CACHE[key]


def _wqk_blocks(W):
    return np.ascontiguousarray(
        W.reshape(16, 128, 16, 128).transpose(2, 1, 0, 3).reshape(H, H))


def _wv_blocks(W):
    return np.ascontiguousarray(
        W.reshape(16, 128, 2, 1024).transpose(2, 0, 1, 3).reshape(2 * H, H // 2))


def _tbl_blocks(t):
    t = np.ascontiguousarray(t, np.float32)
    return np.ascontiguousarray(
        t.reshape(16, 128, 2, TT).transpose(0, 2, 1, 3).reshape(2 * H, TT))


def _make_in_maps(L0_attn_X, L0_attn_W_Q, L0_attn_W_K, L0_attn_W_V,
                  L0_ffn_w1, L0_ffn_w2, L0_ffn_w3):
    X = np.asarray(L0_attn_X, dtype=np.float32)
    WQ = np.ascontiguousarray(np.asarray(L0_attn_W_Q, dtype=np.float32))
    WK = np.ascontiguousarray(np.asarray(L0_attn_W_K, dtype=np.float32))
    WV = np.ascontiguousarray(np.asarray(L0_attn_W_V, dtype=np.float32))
    w1T = np.ascontiguousarray(np.asarray(L0_ffn_w1, np.float32).T)
    w2T = np.ascontiguousarray(np.asarray(L0_ffn_w2, np.float32).T)
    w3T = np.ascontiguousarray(np.asarray(L0_ffn_w3, np.float32).T)
    cqT, sqT, ckT, skT, D = _host_tables()
    ident = np.eye(128, dtype=np.float32)

    w_fl = [w1T.reshape(8, 128, WSL), w2T.reshape(8, 128, WSL),
            w3T.reshape(8, 128, WSL)]

    in_maps = []
    for c in range(8):
        b, par = c // 2, c % 2
        xT = np.ascontiguousarray(X[b, par::2, :].T)
        Dq = D[par::2, :]
        dtt = np.empty((2, 8, 128, NTOK), np.float32)
        for kh in range(2):
            dtt[kh] = np.ascontiguousarray(Dq[:, kh::2].T).reshape(8, 128, NTOK)
        m = {
            "xT": xT,
            "w0sl": np.ascontiguousarray(w_fl[0][c]),
            "w1sl": np.ascontiguousarray(w_fl[1][c]),
            "w2sl": np.ascontiguousarray(w_fl[2][c]),
            "WQ": _wqk_blocks(WQ), "WK": _wqk_blocks(WK),
            "WV": _wv_blocks(WV),
            "cq": _tbl_blocks(cqT[:, par::2]),
            "sq": _tbl_blocks(sqT[:, par::2]),
            "ck": _tbl_blocks(ckT[:, par::2]),
            "sk": _tbl_blocks(skT[:, par::2]),
            "dtt": dtt,
            "ident": ident,
            "kv_sel": np.array([[256 * b]], np.int32),
        }
        if _QWIN:
            import ml_dtypes
            for wi, wT in enumerate((w1T, w2T, w3T)):
                g = np.abs(wT).mean(dtype=np.float64)
                q = np.clip(np.round(wT / (np.float32(g) + np.float32(EPS_Q))),
                            -1, 1).astype(ml_dtypes.bfloat16)
                m[f"q{wi}win"] = q
        in_maps.append(m)
    return in_maps


def kernel(L0_attn_X, L0_attn_W_Q, L0_attn_W_K, L0_attn_W_V,
           L0_ffn_w1, L0_ffn_w2, L0_ffn_w3):
    in_maps = _make_in_maps(L0_attn_X, L0_attn_W_Q, L0_attn_W_K, L0_attn_W_V,
                            L0_ffn_w1, L0_ffn_w2, L0_ffn_w3)
    nc = _get_nc(_TAPS)
    res = run_bass_kernel_spmd(nc, in_maps, core_ids=list(range(8)))

    out = np.empty((B, S, H), np.float32)
    for c in range(8):
        b, par = c // 2, c % 2
        out[b, par::2, :] = res.results[c]["y"].T
    if _TAPS:
        kernel._tap_results = res.results
    return out


# version 03
# speedup vs baseline: 1.0100x; 1.0100x over previous
import numpy as np
from collections import defaultdict

import concourse.bass as bass
import concourse.mybir as mybir
from concourse.bass_core_rust import IndirectOffsetOnAxis
from concourse.expressions import RuntimeValue
from concourse.bass_utils import run_bass_kernel_spmd

F32 = mybir.dt.float32
BF16 = mybir.dt.bfloat16
OP = mybir.AluOpType
AF = mybir.ActivationFunctionType
AX = mybir.AxisListType

B, S, H, HF = 4, 2048, 2048, 5632
NCH = H // 128
HC = HF // 128
NTOK = 1024
TT = 512
NT = NTOK // TT
WSL = (H * HF) // (8 * 128)
NW = H * HF
EPS_Q = 1e-8
EPS_LN = 1e-5

_TAPS = ()
_CUT = None
_QWIN = False

SWAP32 = [i ^ 1 for i in range(32)]
ENGS = ("sync", "scalar", "tensor", "vector", "gpsimd")


def _host_tables():
    def _mk(xp):
        f32 = xp.float32
        d = H
        half = d // 2
        scale_vec = (xp.arange(0, d, 2, dtype=f32) + f32(0.4 * d)) / f32(1.4 * d)
        power = xp.arange(S, dtype=f32) / f32(512.0)
        scale = scale_vec[None, :] ** power[:, None]
        inv_freq = f32(1.0) / (f32(10000.0) ** (xp.arange(half, dtype=f32) / f32(half)))
        sinusoid = xp.arange(S, dtype=f32)[:, None] * inv_freq[None, :]
        sin = xp.sin(sinusoid)
        cos = xp.cos(sinusoid)
        scale_dn = f32(1.0) / scale

        def rep2(a):
            return xp.repeat(a, 2, axis=-1)
        sin_q = rep2(sin * scale)
        cos_q = rep2(cos * scale)
        sin_k = rep2(sin * scale_dn)
        cos_k = rep2(cos * scale_dn)
        n = xp.arange(S)[:, None]
        m = xp.arange(S)[None, :]
        dm = xp.maximum(n - m, 0).astype(f32)
        D = xp.where(n >= m, f32(0.96875) ** dm, f32(0.0))
        return sin_q, cos_q, sin_k, cos_k, D

    try:
        import jax
        import jax.numpy as jnp
        cpu = jax.devices("cpu")[0]
        with jax.default_device(cpu):
            outs = tuple(np.asarray(o) for o in _mk(jnp))
    except Exception:
        outs = tuple(np.asarray(o) for o in _mk(np))
    sin_q, cos_q, sin_k, cos_k, D = outs
    sgn = np.ones((H,), np.float32)
    sgn[0::2] = -1.0
    sin_q = sin_q * sgn[None, :]
    sin_k = sin_k * sgn[None, :]
    return (np.ascontiguousarray(cos_q.T), np.ascontiguousarray(sin_q.T),
            np.ascontiguousarray(cos_k.T), np.ascontiguousarray(sin_k.T), D)


class Plan:

    def __init__(self, nc):
        self.nc = nc
        self.cnt = defaultdict(int)
        self.ops = {e: [] for e in ENGS}
        self.sems = {}

    def sem(self, name):
        if name not in self.sems:
            self.sems[name] = self.nc.alloc_semaphore(name=name)
        return self.sems[name]

    def emit(self, eng, fn, waits=(), inc=None, incn=1):
        ws = [w for w in waits if w is not None and w[1] and w[1] > 0]
        incs = []
        tok = None
        if inc is not None:
            self.cnt[inc] += incn
            tok = self.cnt[inc]
            incs.append((inc, incn))
        self.ops[eng].append((ws, fn, incs))
        return tok

    def tok(self, sem):
        return self.cnt[sem]

    def realize(self, eng_name, eng):
        lastw = {}
        for ws, fn, incs in self.ops[eng_name]:
            for s, v in ws:
                if lastw.get(s, -1) >= v:
                    continue
                lastw[s] = v
                eng.wait_ge(self.sem(s), v)
            inst = fn(eng)
            for s, n in incs:
                inst.then_inc(self.sem(s), n)


def _build_nc(taps=(), cut=None, qwin=False):
    nc = bass.Bass("TRN2", target_bir_lowering=False, detect_race_conditions=False)

    d_xT = nc.dram_tensor("xT", [H, NTOK], F32, kind="ExternalInput").ap()
    d_wsl = [nc.dram_tensor(f"w{i}sl", [128, WSL], F32, kind="ExternalInput").ap()
             for i in range(3)]
    d_WQ = nc.dram_tensor("WQ", [H, H], F32, kind="ExternalInput").ap()
    d_WK = nc.dram_tensor("WK", [H, H], F32, kind="ExternalInput").ap()
    d_WV = nc.dram_tensor("WV", [2 * H, H // 2], F32, kind="ExternalInput").ap()
    d_cq = nc.dram_tensor("cq", [2 * H, TT], F32, kind="ExternalInput").ap()
    d_sq = nc.dram_tensor("sq", [2 * H, TT], F32, kind="ExternalInput").ap()
    d_ck = nc.dram_tensor("ck", [2 * H, TT], F32, kind="ExternalInput").ap()
    d_sk = nc.dram_tensor("sk", [2 * H, TT], F32, kind="ExternalInput").ap()
    d_dt = nc.dram_tensor("dtt", [2, 8, 128, NTOK], F32, kind="ExternalInput").ap()
    d_id = nc.dram_tensor("ident", [128, 128], F32, kind="ExternalInput").ap()
    d_y = nc.dram_tensor("y", [H, NTOK], F32, kind="ExternalOutput").ap()

    d_gin = nc.dram_tensor("g_in", [1, 4], F32)
    d_gout = nc.dram_tensor("g_out", [1, 4], F32)
    d_bin = nc.dram_tensor("b_in", [1, 4], F32)
    d_bout = nc.dram_tensor("b_out", [1, 4], F32)
    d_bin2 = nc.dram_tensor("b_in2", [1, 4], F32)
    d_bout2 = nc.dram_tensor("b_out2", [1, 4], F32)
    d_qin = [nc.dram_tensor(f"q{i}in", [128, WSL], BF16) for i in range(3)]
    d_qw = [nc.dram_tensor("q0w", [H, HF], BF16),
            nc.dram_tensor("q1w", [H, HF], BF16),
            nc.dram_tensor("q2w", [HF, H], BF16)]
    d_qwb = [nc.dram_tensor("q0wb", [H, HF], BF16),
             nc.dram_tensor("q1wb", [H, HF], BF16),
             nc.dram_tensor("q2wb", [HF, H], BF16)]
    d_qwc = d_qwb
    if qwin:
        d_qwc = [nc.dram_tensor("q0win", [H, HF], BF16, kind="ExternalInput"),
                 nc.dram_tensor("q1win", [H, HF], BF16, kind="ExternalInput"),
                 nc.dram_tensor("q2win", [HF, H], BF16, kind="ExternalInput")]
    d_kvs = nc.dram_tensor("kv_s", [128, 32768], BF16)
    d_kvag = nc.dram_tensor("kv_ag", [1024, 32768], BF16)
    d_kvrb = nc.dram_tensor("kv_rb", [256, 32768], BF16)
    d_kvsel = nc.dram_tensor("kv_sel", [1, 1], mybir.dt.int32, kind="ExternalInput").ap()

    tap_ap = {}
    for t in list(taps) + (["thr_gr", "thr_qred"] if "thr" in taps else []):
        shp, dt = {"xln": ([128, NCH * TT], BF16), "qf": ([128, NCH * NTOK], BF16),
                   "x1": ([128, NCH * NTOK], F32), "x2": ([128, NCH * NTOK], F32),
                   "att": ([128, 9216], BF16), "kvs": ([128, 32768], BF16),
                   "hid": ([128, HC * TT], BF16), "qw0": ([H, HF], BF16), "kvrb": ([256, 32768], BF16),
                   "thr": ([128, 8], F32), "thr_gr": ([1, 16], F32), "lnmr": ([128, 1024], F32), "qw0s": ([128, 1024], BF16), "k1dbg": ([128, 3072], BF16),
                   "thr_qred": ([128, 40], F32)}[t]
        tap_ap[t] = nc.dram_tensor(f"tap_{t}", shp, dt, kind="ExternalOutput").ap()

    A = nc.alloc_sbuf_tensor
    s_xf = A("xf", [128, NCH * NTOK], F32)
    s_xln = A("xlnb", [128, NCH * TT], BF16)
    s_U = A("U", [128, 32768], BF16)
    s_att = A("attsb", [128, 9216], BF16)
    s_wr = A("wring", [128, 8 * TT], BF16)
    s_S1 = A("S1", [128, 4096], BF16)
    s_S2 = A("S2", [128, 3072], F32)
    s_id = A("idsb", [128, 128], F32)
    s_on = A("onescol", [128, 4], F32)
    s_onb = A("onescolb", [128, 1], BF16)
    s_orow = A("onesrow", [1, 128], F32)
    s_thr = A("thr", [128, 8], F32)
    s_gr = A("grow", [1, 16], F32)
    s_kvix = A("kvix", [1, 1], mybir.dt.int32)

    U32 = s_U.ap().bitcast(F32)
    v_qbuf = [U32[:, 0:1408], U32[:, 1408:2816]]
    v_qout = s_U[:, 8192:8192 + WSL]
    S2b = s_S2.ap().bitcast(BF16)
    v_kev = [S2b[:, 0:512], S2b[:, 512:1024]]
    v_kev2 = [S2b[:, 1024:1536], S2b[:, 1536:2048]]
    v_vev = [S2b[:, 2048:3072], S2b[:, 3072:4096]]
    v_kout = [S2b[:, 4096:4608], S2b[:, 4608:5120]]
    v_mrbc = s_S2[:, 0:1024]
    v_m = s_S2[0:1, 1024:1536]
    v_q2 = s_S2[0:1, 1536:2048]
    v_ac = s_S2[0:1, 2048:3072]
    v_sg = s_S1[:, 0:2048]
    v_xbsc = s_S1[:, 2048:4096]
    v_xscr = v_xbsc.bitcast(F32)
    v_qredv = s_S2[:, 0:24]
    s_S3 = s_S2[:, 0:2048]
    s_db = s_U.ap().bitcast(F32)[:, 8192:10240]
    s_vb = s_U[:, 20480:24576]
    s_kb = s_U[:, 24576:28672]
    v_qredw = s_S2[:, 32:35]
    v_qn = s_S1[:, 0:1408]
    v_wqb = [s_S1[:, 0:2048], s_S1[:, 2048:4096]]
    v_wqf = [U32[:, 8192:10240], U32[:, 10240:12288]]
    attF = s_att.ap().bitcast(F32)
    v_tbf = [attF[:, 0:1024], attF[:, 1024:2048]]

    PS = nc.alloc_psum_tensor("PS", [128, 4096], F32)

    def bank(i, n0=0, w=512):
        return PS[:, i * 512 + n0:i * 512 + n0 + w]

    P = Plan(nc)
    E = P.emit
    for s in ("ls", "lg", "sa", "cc", "mm", "vv", "aa"):
        P.sem(s)

    bankfree = {i: None for i in range(8)}

    E("vector", lambda v: v.memset(s_on[:, :], 1.0))
    E("vector", lambda v: v.memset(s_onb[:, :], 1.0))
    E("vector", lambda v: v.memset(s_orow[:, :], 1.0))
    E("vector", lambda v: v.memset(s_gr[:, :], 0.0), inc="vv")
    E("gpsimd", lambda g: g.dma_start(out=s_id[:, :], in_=d_id[:, :]), inc="lg", incn=16)
    for dc in range(NCH):
        E("gpsimd", lambda g, dc=dc: g.dma_start(
            out=s_xf[:, dc * NTOK:(dc + 1) * NTOK],
            in_=d_xT[dc * 128:(dc + 1) * 128, :]), inc="lg", incn=16)
    t_xload = P.tok("lg")

    NQT = 8
    QW = WSL // NQT
    red_tok = []
    for wi in range(3):
        for k in range(NQT):
            i = wi * NQT + k
            sl = i % 2
            wprev = ("vv", red_tok[i - 2]) if i >= 2 else None
            E("sync", lambda s, wi=wi, k=k, sl=sl: s.dma_start(
                out=v_qbuf[sl], in_=d_wsl[wi][:, k * QW:(k + 1) * QW]),
                waits=[wprev], inc="ls", incn=16)
            tl = P.tok("ls")
            E("vector", lambda v, i=i, sl=sl: v.tensor_reduce(
                v_qredv[:, i:i + 1], v_qbuf[sl], AX.X, OP.add, True),
                waits=[("ls", tl)], inc="vv")
            red_tok.append(P.tok("vv"))
    for wi in range(3):
        E("vector", lambda v, wi=wi: v.tensor_reduce(
            v_qredw[:, wi:wi + 1], v_qredv[:, wi * NQT:(wi + 1) * NQT],
            AX.X, OP.add, False), inc="vv")
    t_qred = P.tok("vv")
    E("tensor", lambda t: t.matmul(PS[0:1, 3072:3075], s_on[:, 0:1], v_qredw[:, 0:3],
                                   start=True, stop=True),
      waits=[("vv", t_qred)], inc="mm")
    t_gmm = P.tok("mm")
    E("vector", lambda v: v.tensor_copy(s_gr[0:1, 0:3], PS[0:1, 3072:3075]),
      waits=[("mm", t_gmm)], inc="vv")
    t_gcp = P.tok("vv")
    E("gpsimd", lambda g: g.dma_start(out=d_gin.ap()[0:1, 0:4], in_=s_gr[0:1, 0:4]),
      waits=[("vv", t_gcp)], inc="lg", incn=16)
    t_gw = P.tok("lg")
    E("gpsimd", lambda g: g.collective_compute(
        "AllReduce", OP.add, replica_groups=[list(range(8))],
        ins=[d_gin.ap().opt()], outs=[d_gout.ap().opt()]),
        waits=[("lg", t_gw)], inc="cc")
    t_ar = P.tok("cc")
    E("gpsimd", lambda g: g.dma_start(out=s_gr[0:1, 4:8], in_=d_gout.ap()[0:1, 0:4]),
      waits=[("cc", t_ar)], inc="lg", incn=16)
    t_grl = P.tok("lg")
    c1 = float(np.float32(0.5) / np.float32(NW))
    c2 = float(0.5 * EPS_Q)
    E("tensor", lambda t: t.matmul(PS[:, 3072:3075], s_orow[0:1, :], s_gr[0:1, 4:7],
                                   start=True, stop=True),
      waits=[("lg", t_grl)], inc="mm")
    t_tbc = P.tok("mm")
    E("vector", lambda v: v.tensor_scalar(s_thr[:, 0:3], PS[:, 3072:3075], c1, c2,
                                          OP.mult, OP.add),
      waits=[("mm", t_tbc)])
    E("vector", lambda v: v.tensor_scalar(s_thr[:, 3:6], PS[:, 3072:3075], -c1, -c2,
                                          OP.mult, OP.add), inc="vv")
    t_thr = P.tok("vv")
    bankfree[6] = bankfree[7] = ("vv", t_thr)

    t_qag = []
    qv_tok = []
    qw_tok = {}
    for wi in range(3):
        t_last = 0
        for k in range(NQT):
            i = wi * NQT + k
            sl = i % 2
            wprev = ("vv", qv_tok[i - 2]) if i >= 2 else ("vv", t_thr)
            E("sync", lambda s, wi=wi, k=k, sl=sl: s.dma_start(
                out=v_qbuf[sl], in_=d_wsl[wi][:, k * QW:(k + 1) * QW]),
                waits=[wprev, ("cc", t_ar)], inc="ls", incn=16)
            tl = P.tok("ls")
            qp = v_qout[:, k * QW:(k + 1) * QW]
            wq = ("lg", qw_tok[(wi - 1, k)]) if wi > 0 else None
            E("vector", lambda v, qp=qp, sl=sl, wi=wi: v.tensor_scalar(
                qp, v_qbuf[sl], s_thr[:, wi:wi + 1], None, OP.is_gt),
                waits=[("ls", tl), ("vv", t_thr), wq])
            E("vector", lambda v, sl=sl, wi=wi: v.tensor_scalar(
                v_qn, v_qbuf[sl], s_thr[:, 3 + wi:4 + wi], None, OP.is_lt))
            E("vector", lambda v, qp=qp: v.tensor_sub(qp, qp, v_qn), inc="vv")
            qv_tok.append(P.tok("vv"))
            E("gpsimd", lambda g, wi=wi, k=k, qp=qp: g.dma_start(
                out=d_qin[wi][:, k * QW:(k + 1) * QW], in_=qp),
                waits=[("vv", qv_tok[-1])], inc="lg", incn=16)
            t_last = P.tok("lg")
            qw_tok[(wi, k)] = t_last
    for wi in range(3):
        E("gpsimd", lambda g, wi=wi: g.collective_compute(
            "AllGather", OP.bypass, replica_groups=[list(range(8))],
            ins=[d_qin[wi].ap().opt()], outs=[d_qw[wi].ap().opt()]),
            waits=[("lg", t_last)], inc="cc")
    E("gpsimd", lambda g: g.collective_compute(
        "AllReduce", OP.add, replica_groups=[list(range(8))],
        ins=[d_bin.ap().opt()], outs=[d_bout.ap().opt()]), inc="cc")
    t_bar = P.tok("cc")
    snapA = [[None] * NCH, [None] * NCH]
    snapB = [None] * HC
    for dc in range(NCH):
        for wi in range(2):
            E("gpsimd", lambda g, wi=wi, dc=dc: g.dma_start(
                out=d_qwb[wi].ap()[dc * 128:(dc + 1) * 128, :],
                in_=d_qw[wi].ap()[dc * 128:(dc + 1) * 128, :]),
                waits=[("cc", t_bar)], inc="lg", incn=16)
            snapA[wi][dc] = ("lg", P.tok("lg"))
    for hc in range(HC):
        E("gpsimd", lambda g, hc=hc: g.dma_start(
            out=d_qwb[2].ap()[hc * 128:(hc + 1) * 128, :],
            in_=d_qw[2].ap()[hc * 128:(hc + 1) * 128, :]),
            waits=[("cc", t_bar)], inc="lg", incn=16)
        snapB[hc] = ("lg", P.tok("lg"))
    t_qag.extend([snapA[0][NCH - 1], snapA[1][NCH - 1], snapB[HC - 1]])
    if "thr" in tap_ap:
        E("gpsimd", lambda g: g.dma_start(out=tap_ap["thr"], in_=s_thr[:, :]),
          waits=[("vv", t_thr)], inc="lg", incn=16)
        E("gpsimd", lambda g: g.dma_start(out=tap_ap["thr_gr"], in_=s_gr[:, :]),
          waits=[("vv", t_thr)], inc="lg", incn=16)
        E("gpsimd", lambda g: g.dma_start(out=tap_ap["thr_qred"][:, 0:35], in_=s_S2[:, 0:35]),
          waits=[("vv", t_thr)], inc="lg", incn=16)
    if "qw0" in tap_ap:
        E("sync", lambda s: s.dma_start(out=tap_ap["qw0"], in_=d_qw[0].ap()),
          waits=[t_qag[2]], inc="ls", incn=16)


    lnslot = [0, 0]

    def ln_tile(tt, extra_waits=()):
        fence = [("mm", P.tok("mm")), ("lg", P.tok("lg")), ("aa", P.tok("aa"))]
        for dc in range(NCH):
            sl = dc % 2
            xsl = s_xf[:, dc * NTOK + tt * TT: dc * NTOK + tt * TT + TT]
            xb = v_xbsc[:, sl * 512:sl * 512 + 512]
            xq = v_xbsc[:, 1024 + sl * 512:1024 + sl * 512 + 512]
            ws = (list(extra_waits) + fence) if dc == 0 else [("mm", lnslot[sl])]
            E("vector", lambda v, xb=xb, xsl=xsl: v.tensor_copy(xb, xsl), waits=ws)
            E("vector", lambda v, xb=xb, xq=xq: v.tensor_mul(xq, xb, xb), inc="vv")
            tv = P.tok("vv")
            ws2 = [("vv", tv)]
            if dc == 0:
                ws2 += [bankfree[6], bankfree[7]]
            E("tensor", lambda t, xb=xb, dc=dc: t.matmul(
                PS[0:1, 3072:3584], s_onb[:, 0:1], xb,
                start=(dc == 0), stop=(dc == NCH - 1)), waits=ws2)
            E("tensor", lambda t, xq=xq, dc=dc: t.matmul(
                PS[0:1, 3584:4096], s_onb[:, 0:1], xq,
                start=(dc == 0), stop=(dc == NCH - 1)), inc="mm")
            lnslot[sl] = P.tok("mm")
        t_st = P.tok("mm")
        cN = float(np.float32(1.0) / np.float32(H))
        E("vector", lambda v: v.tensor_copy(v_m, PS[0:1, 3072:3584]),
          waits=[("mm", t_st)])
        E("vector", lambda v: v.tensor_copy(v_q2, PS[0:1, 3584:4096]), inc="vv")
        tv = P.tok("vv")
        E("tensor", lambda t: t.matmul(PS[:, 3072:3584], s_orow[0:1, :], v_m,
                                       start=True, stop=True), waits=[("vv", tv)])
        E("tensor", lambda t: t.matmul(PS[:, 3584:4096], s_orow[0:1, :], v_q2,
                                       start=True, stop=True), inc="mm")
        tm = P.tok("mm")
        sA = v_xscr[:, 0:512]
        sB = v_xscr[:, 512:1024]
        E("vector", lambda v: v.tensor_scalar(v_mrbc[:, 512:1024], PS[:, 3072:3584],
                                              cN, None, OP.mult), waits=[("mm", tm)])
        E("vector", lambda v: v.tensor_scalar(sA, PS[:, 3584:4096], cN, None, OP.mult),
          inc="vv")
        t1_ = P.tok("vv")
        bankfree[6] = bankfree[7] = ("vv", t1_)
        E("vector", lambda v: v.tensor_mul(sB, v_mrbc[:, 512:1024],
                                           v_mrbc[:, 512:1024]),
          waits=[("vv", t1_)], inc="vv")
        t2_ = P.tok("vv")
        E("vector", lambda v: v.tensor_sub(sA, sA, sB), waits=[("vv", t2_)], inc="vv")
        t3_ = P.tok("vv")
        E("vector", lambda v: v.tensor_scalar(sA, sA, EPS_LN, None, OP.add),
          waits=[("vv", t3_)], inc="vv")
        t4_ = P.tok("vv")
        E("scalar", lambda a: a.activation(sB, sA, AF.Ln), waits=[("vv", t4_)],
          inc="aa")
        t5_ = P.tok("aa")
        E("scalar", lambda a: a.activation(v_mrbc[:, 0:512], sB, AF.Exp, scale=-0.5),
          waits=[("aa", t5_)], inc="aa")
        t_exp = P.tok("aa")
        for dc in range(NCH):
            sl = dc % 2
            xsl = s_xf[:, dc * NTOK + tt * TT: dc * NTOK + tt * TT + TT]
            u = v_xscr[:, sl * 512:sl * 512 + 512]
            ws3 = [("aa", t_exp)] if dc == 0 else []
            E("vector", lambda v, u=u, xsl=xsl: v.tensor_tensor(
                u, xsl, v_mrbc[:, 512:1024], OP.subtract), waits=ws3)
            E("vector", lambda v, u=u, dc=dc: v.tensor_tensor(
                s_xln[:, dc * TT:(dc + 1) * TT], u, v_mrbc[:, 0:512], OP.mult),
                inc="vv")
        return P.tok("vv")

    wr_n = [0]
    wr_free = [None] * 8

    def wr_load(src_fn, extra=None):
        slot = wr_n[0] % 8
        E("sync", lambda s, slot=slot: s.dma_start(
            out=s_wr[:, slot * TT:(slot + 1) * TT], in_=src_fn()),
            waits=[wr_free[slot], extra], inc="ls", incn=16)
        wr_n[0] += 1
        return slot, P.tok("ls")

    def ffn(app, tt, xln_tok):
        for g in range(11):
            stop_tok = {}
            for which in (0, 1):
                base = 0 if which == 0 else 4
                for dc in range(NCH):
                    slot, tl = wr_load(lambda wi=which, g=g, dc=dc: d_qwc[wi].ap()[
                        dc * 128:(dc + 1) * 128, g * 512:(g + 1) * 512],
                        extra=snapA[which][dc])
                    tok = None
                    for k in range(4):
                        bk = base + k
                        ws = []
                        if k == 0:
                            ws.append(("ls", tl))
                        if dc == 0:
                            ws.append(bankfree[bk])
                            if k == 0:
                                ws.append(("vv", xln_tok))
                        tok = E("tensor", lambda t, slot=slot, k=k, dc=dc, bk=bk:
                                t.matmul(
                            bank(bk),
                            s_wr[:, slot * TT + k * 128: slot * TT + (k + 1) * 128],
                            s_xln[:, dc * TT:(dc + 1) * TT],
                            start=(dc == 0), stop=(dc == NCH - 1)),
                            waits=ws, inc="mm")
                        if dc == NCH - 1:
                            stop_tok[bk] = tok
                    wr_free[slot] = ("mm", tok)
            for k in range(4):
                E("scalar", lambda a, k=k: a.activation(
                    v_sg[:, k * 512:(k + 1) * 512], bank(k), AF.Sigmoid),
                    waits=[("mm", stop_tok[k])], inc="aa")
                bankfree[k] = ("aa", P.tok("aa"))
            ta = P.tok("aa")
            for k in range(4):
                hc = 4 * g + k
                E("vector", lambda v, k=k, hc=hc: v.tensor_tensor(
                    s_U[:, hc * TT:(hc + 1) * TT], v_sg[:, k * 512:(k + 1) * 512],
                    bank(4 + k), OP.mult),
                    waits=[("aa", ta), ("mm", stop_tok[4 + k])], inc="vv")
                bankfree[4 + k] = ("vv", P.tok("vv"))
        t_hid = P.tok("vv")
        if "hid" in tap_ap and app == 0 and tt == 0:
            E("gpsimd", lambda g: g.dma_start(out=tap_ap["hid"], in_=s_U[:, 0:HC * TT]),
              waits=[("vv", t_hid)], inc="lg", incn=16)
        for dg in range(4):
            stop_tok = {}
            for hc in range(HC):
                slot, tl = wr_load(lambda dg=dg, hc=hc: d_qwc[2].ap()[
                    hc * 128:(hc + 1) * 128, dg * 512:(dg + 1) * 512],
                    extra=snapB[hc])
                tok = None
                for k in range(4):
                    ws = []
                    if k == 0:
                        ws.append(("ls", tl))
                    if hc == 0:
                        ws.append(bankfree[k])
                        if k == 0 and dg == 0:
                            ws.append(("vv", t_hid))
                    tok = E("tensor", lambda t, slot=slot, k=k, hc=hc: t.matmul(
                        bank(k),
                        s_wr[:, slot * TT + k * 128: slot * TT + (k + 1) * 128],
                        s_U[:, hc * TT:(hc + 1) * TT],
                        start=(hc == 0), stop=(hc == HC - 1)),
                        waits=ws, inc="mm")
                    if hc == HC - 1:
                        stop_tok[k] = tok
                wr_free[slot] = ("mm", tok)
            for k in range(4):
                dc = 4 * dg + k
                xsl = s_xf[:, dc * NTOK + tt * TT: dc * NTOK + tt * TT + TT]
                E("vector", lambda v, k=k, xsl=xsl: v.scalar_tensor_tensor(
                    xsl, bank(k), 0.5, xsl, OP.mult, OP.add),
                    waits=[("mm", stop_tok[k])], inc="vv")
                bankfree[k] = ("vv", P.tok("vv"))
                if app == 1:
                    tv = P.tok("vv")
                    E("scalar", lambda a, dc=dc, xsl=xsl: a.dma_start(
                        out=d_y[dc * 128:(dc + 1) * 128, tt * TT:(tt + 1) * TT],
                        in_=xsl), waits=[("vv", tv)], inc="sa", incn=16)
        return P.tok("vv")

    def _finish_early():
        tv = P.tok("vv")
        ta = P.tok("aa")
        tm = P.tok("mm")
        for dc in range(NCH):
            E("scalar", lambda a, dc=dc: a.dma_start(
                out=d_y[dc * 128:(dc + 1) * 128, :],
                in_=s_xf[:, dc * NTOK:(dc + 1) * NTOK]),
                waits=[("vv", tv), ("aa", ta), ("mm", tm), ("lg", P.tok("lg")),
                       ("ls", P.tok("ls")), ("cc", P.tok("cc"))],
                inc="sa", incn=16)
        E("scalar", lambda a: a.nop(), waits=[("sa", P.tok("sa"))])

    def _realize():
        with nc.Block() as block:
            @block.sync
            def _(e):
                P.realize("sync", e)

            @block.scalar
            def _(e):
                P.realize("scalar", e)

            @block.tensor
            def _(e):
                P.realize("tensor", e)

            @block.vector
            def _(e):
                P.realize("vector", e)

            @block.gpsimd
            def _(e):
                P.realize("gpsimd", e)

    if cut == "quant":
        _finish_early(); _realize(); return nc

    tok = ln_tile(0, extra_waits=[("lg", t_xload)])
    tok_f0 = ffn(0, 0, tok)
    tok = ln_tile(1)
    if "lnmr" in tap_ap:
        E("gpsimd", lambda g: g.dma_start(out=tap_ap["lnmr"], in_=v_mrbc[:, 0:1024]),
          waits=[("vv", tok)], inc="lg", incn=16)
    if "xln" in tap_ap and cut == "ffn1":
        E("gpsimd", lambda g: g.dma_start(out=tap_ap["xln"], in_=s_xln[:, :]),
          waits=[("vv", tok)], inc="lg", incn=16)
    tok_f1 = ffn(0, 1, tok)
    t_ffn1 = max(tok_f0, tok_f1)

    if "x1" in tap_ap:
        E("gpsimd", lambda g: g.dma_start(out=tap_ap["x1"], in_=s_xf[:, :]),
          waits=[("vv", t_ffn1)], inc="lg", incn=16)

    if cut == "ffn1":
        _finish_early(); _realize(); return nc

    wq_free = [None, None]
    tb_free = [None, None]
    kout_free = [None, None]
    vev_free = [None, None]
    kvg_wb = [None, None]
    pbank = [0]
    proj_last = [0]

    def _kvs_k_dst(jc, tt):
        base = tt * 4 * 2048 + jc * 128
        ap = d_kvs.ap()
        flat = ap[:, base:base + 3 * 2048 + 128]
        return bass.AP(flat.tensor, flat.offset,
                       [list(flat.ap[0]), [2048, 4], [1, 128]])

    def proj_qk(which, tt, xln_tok):
        dW = d_WQ if which == "q" else d_WK
        dcos = d_cq if which == "q" else d_ck
        dsin = d_sq if which == "q" else d_sk
        def _wload(jc):
            E("gpsimd", lambda g, jc=jc: g.dma_start(
                out=v_wqf[jc % 2], in_=dW[jc * 128:(jc + 1) * 128, :]),
                waits=[wq_free[jc % 2], ("vv", t_ffn1)], inc="lg", incn=16)
            return P.tok("lg")
        _wload(0)
        for jc in range(NCH):
            sl = jc % 2
            wsl_ = v_wqb[sl]
            wfs = v_wqf[sl]
            tlwf = _wload(jc + 1) if jc + 1 < NCH else P.tok("lg")
            E("vector", lambda v, wsl_=wsl_, wfs=wfs: v.scalar_tensor_tensor(
                wsl_, wfs, 1.0, wfs, OP.mult, OP.max),
                waits=[("lg", tlwf), wq_free[sl]], inc="vv")
            tlw = ("vv", P.tok("vv"))
            ctb = v_tbf[sl][:, 0:512]
            stb = v_tbf[sl][:, 512:1024]
            blk = jc * 2 + tt
            E("scalar", lambda a, ctb=ctb, dcos=dcos, blk=blk: a.dma_start(
                out=ctb, in_=dcos[blk * 128:(blk + 1) * 128, :]),
                waits=[tb_free[sl]], inc="sa", incn=16)
            E("scalar", lambda a, stb=stb, dsin=dsin, blk=blk: a.dma_start(
                out=stb, in_=dsin[blk * 128:(blk + 1) * 128, :]),
                inc="sa", incn=16)
            tlt = P.tok("sa")
            bk = pbank[0] % 4
            pbank[0] += 1
            ws0 = [tlw, bankfree[bk]]
            if jc == 0:
                ws0.append(("vv", xln_tok))
            for dc in range(NCH):
                E("tensor", lambda t, wsl_=wsl_, dc=dc, bk=bk: t.matmul(
                    bank(bk), wsl_[:, dc * 128:(dc + 1) * 128],
                    s_xln[:, dc * TT:(dc + 1) * TT],
                    start=(dc == 0), stop=(dc == NCH - 1)),
                    waits=(ws0 if dc == 0 else []), inc="mm")
            tmm = P.tok("mm")
            wq_free[sl] = ("mm", tmm)
            kb, rot = v_kev[0], v_kev[1]
            tb_, ub = v_kev2[0], v_kev2[1]
            if which == "q":
                out = s_U[:, jc * NTOK + tt * TT: jc * NTOK + tt * TT + TT]
            else:
                out = v_kout[1 - sl]
            ws1 = [("mm", tmm), ("sa", tlt), ("vv", proj_last[0])]
            if which == "k":
                ws1.append(kout_free[1 - sl])
            E("vector", lambda v, kb=kb, bk=bk: v.tensor_copy(kb, bank(bk)),
              waits=ws1, inc="vv")
            e1 = P.tok("vv")
            E("vector", lambda v, rot=rot, kb=kb: v.stream_shuffle(rot, kb, SWAP32),
              waits=[("vv", e1)], inc="vv")
            e2 = P.tok("vv")
            E("vector", lambda v, tb_=tb_, rot=rot, stb=stb: v.tensor_mul(tb_, rot, stb),
              waits=[("vv", e2)], inc="vv")
            e3 = P.tok("vv")
            E("vector", lambda v, ub=ub, kb=kb, ctb=ctb: v.tensor_mul(ub, kb, ctb),
              waits=[("vv", e3)], inc="vv")
            e4 = P.tok("vv")
            bankfree[bk] = ("vv", e4)
            tb_free[sl] = ("vv", e4)
            if which == "q":
                E("vector", lambda v, out=out, ub=ub, tb_=tb_: v.tensor_add(out, ub, tb_),
                  waits=[("vv", e4)], inc="vv")
            else:
                E("vector", lambda v, out=out, ub=ub, tb_=tb_: v.tensor_add(out, ub, tb_),
                  waits=[("vv", e4)], inc="vv")
            proj_last[0] = P.tok("vv")
            if which == "k":
                if "k1dbg" in tap_ap and jc == 1 and tt == 0:
                    E("gpsimd", lambda g: g.dma_start(
                        out=tap_ap["k1dbg"][:, 2048:3072],
                        in_=s_att[:, 2048:3072]),
                        waits=[("vv", proj_last[0])], inc="lg", incn=16)
                if "k1dbg" in tap_ap and jc in (1, 3):
                    E("gpsimd", lambda g, jc=jc, tt=tt, sl=sl: g.dma_start(
                        out=tap_ap["k1dbg"][:, (jc // 2) * 1024 + tt * TT:
                                            (jc // 2) * 1024 + tt * TT + TT],
                        in_=v_kout[1 - sl]),
                        waits=[("vv", proj_last[0])], inc="lg", incn=16)
                E("sync", lambda s, jc=jc, tt=tt, sl=sl: s.dma_start(
                    out=d_kvs.ap()[:, jc * 1024 + tt * TT: jc * 1024 + tt * TT + TT],
                    in_=v_kout[1 - sl]),
                    waits=[("vv", proj_last[0])], inc="ls", incn=16)
                kout_free[1 - sl] = ("ls", P.tok("ls"))
        return proj_last[0]

    def proj_v(tt, xln_tok):
        def _vload(vblk):
            E("gpsimd", lambda g, vblk=vblk: g.dma_start(
                out=v_wqf[vblk % 2][:, 0:1024],
                in_=d_WV[vblk * 128:(vblk + 1) * 128, :]),
                waits=[wq_free[vblk % 2], ("vv", t_ffn1)], inc="lg", incn=16)
            return P.tok("lg")
        _vload(0)
        for half in range(2):
            stop_tok = {}
            for dc in range(NCH):
                sl = dc % 2
                wsl_ = v_wqb[sl]
                wfs = v_wqf[sl]
                vblk = half * NCH + dc
                tlwf = _vload(vblk + 1) if vblk + 1 < 2 * NCH else P.tok("lg")
                E("vector", lambda v, wsl_=wsl_, wfs=wfs: v.scalar_tensor_tensor(
                    wsl_[:, 0:1024], wfs[:, 0:1024], 1.0, wfs[:, 0:1024],
                    OP.mult, OP.max),
                    waits=[("lg", tlwf), wq_free[sl]], inc="vv")
                tlw = ("vv", P.tok("vv"))
                tok = None
                for tb in range(4):
                    for n in range(2):
                        bk = 2 * tb + n
                        ws = []
                        if tb == 0 and n == 0:
                            ws.append(tlw)
                        if dc == 0:
                            ws.append(bankfree[bk])
                            if tb == 0 and n == 0:
                                ws.append(("vv", xln_tok))
                        tok = E("tensor",
                                lambda t, wsl_=wsl_, dc=dc, tb=tb, n=n, bk=bk: t.matmul(
                            bank(bk),
                            s_xln[:, dc * TT + tb * 128: dc * TT + (tb + 1) * 128],
                            wsl_[:, n * 512:(n + 1) * 512],
                            start=(dc == 0), stop=(dc == NCH - 1)),
                            waits=ws, inc="mm")
                        if dc == NCH - 1:
                            stop_tok[bk] = tok
                wq_free[sl] = ("mm", tok)
            for tb in range(4):
                sl = tb % 2
                E("vector", lambda v, tb=tb, sl=sl: v.tensor_copy(
                    v_vev[sl], PS[:, tb * 1024:(tb + 1) * 1024]),
                    waits=[("mm", stop_tok[2 * tb + 1]), vev_free[sl]], inc="vv")
                tv0 = P.tok("vv")
                E("vector", lambda v, sl=sl: v.tensor_tensor(
                    v_vev[sl], v_vev[sl], v_vev[sl], OP.max),
                    waits=[("vv", tv0)], inc="vv")
                tv = P.tok("vv")
                bankfree[2 * tb] = bankfree[2 * tb + 1] = ("vv", tv)
                tbg = tt * 4 + tb
                E("sync", lambda s, tbg=tbg, half=half, sl=sl: s.dma_start(
                    out=d_kvs.ap()[:, 16384 + tbg * 2048 + half * 1024:
                                   16384 + tbg * 2048 + (half + 1) * 1024],
                    in_=v_vev[sl]), waits=[("vv", tv)], inc="ls", incn=16)
                vev_free[sl] = ("ls", P.tok("ls"))
        return P.tok("lg")

    t_ag_trigger = None
    tokl0 = ln_tile(0, extra_waits=[("vv", t_ffn1)])
    proj_qk("k", 0, tokl0)
    proj_v(0, tokl0)
    proj_qk("q", 0, tokl0)
    tokl1 = ln_tile(1)
    proj_qk("k", 1, tokl1)
    t_kv_done = proj_v(1, tokl1)
    t_q_done = proj_qk("q", 1, tokl1)

    E("gpsimd", lambda g: g.dma_start(out=s_kvix[:, :], in_=d_kvsel[:, :]),
      inc="lg", incn=16)
    t_ixl = P.tok("lg")
    E("gpsimd", lambda g: g.collective_compute(
        "AllGather", OP.bypass, replica_groups=[list(range(8))],
        ins=[d_kvs.ap().opt()], outs=[d_kvag.ap().opt()]),
        waits=[("lg", P.tok("lg")), ("ls", P.tok("ls")), ("sa", P.tok("sa")),
               ("vv", t_q_done)], inc="cc")
    for _ in range(3):
        E("gpsimd", lambda g: g.collective_compute(
            "AllReduce", OP.add, replica_groups=[list(range(8))],
            ins=[d_bin2.ap().opt()], outs=[d_bout2.ap().opt()]), inc="cc")
    t_cc2 = P.tok("cc")
    def _kv_snap(g):
        r = g.alloc_register("kvbase")
        g.reg_load(r, s_kvix[0:1, 0:1])
        sv = RuntimeValue(r, min_val=0, max_val=768)
        return g.dma_start(out=d_kvrb.ap()[:, :],
                           in_=d_kvag.ap()[bass.ds(sv, 256), :])
    E("gpsimd", _kv_snap, waits=[("cc", t_cc2), ("lg", t_ixl)], inc="lg", incn=16)
    t_kvag = ("lg", P.tok("lg"))

    if "kvs" in tap_ap:
        E("gpsimd", lambda g: g.dma_start(out=tap_ap["kvs"], in_=d_kvs.ap()),
          waits=[("lg", t_kv_done)], inc="lg", incn=16)
    if "xln" in tap_ap:
        E("gpsimd", lambda g: g.dma_start(out=tap_ap["xln"], in_=s_xln[:, :]),
          waits=[("vv", t_q_done)], inc="lg", incn=16)
    if "qf" in tap_ap:
        E("gpsimd", lambda g: g.dma_start(out=tap_ap["qf"], in_=s_U[:, 0:NCH * NTOK]),
          waits=[("vv", t_q_done)], inc="lg", incn=16)

    if cut == "proj":
        _finish_early(); _realize(); return nc

    def nq(j):
        return NTOK - 128 * j

    cb = {}
    off = 0
    for kh in range(2):
        for j in range(8):
            cb[(kh, j)] = off
            off += nq(j)
    assert off == 9216

    kb_free = [None, None]
    db_free = [None, None]
    vb_free = [None, None]
    att_tok = {}
    abank_n = [0]
    vb_n = [0]
    ab_n = [0]
    tp_free = [None] * 2
    avac_free = [None]

    def _kvrb_k_src(kh, j):
        base = kh * 128 * 32768 + (j // 4) * TT + (j % 4) * 128
        ap = d_kvrb.ap()
        return bass.AP(ap.tensor, base, [[32768, 128], [1024, NCH], [1, 128]])

    def phase_a(kh, j):
        sl = ab_n[0] % 2
        ab_n[0] += 1
        E("sync", lambda s, kh=kh, j=j, sl=sl: s.dma_start(
            out=s_kb[:, sl * 2048:(sl + 1) * 2048].rearrange("p (c u) -> p c u", c=NCH),
            in_=_kvrb_k_src(kh, j)),
            waits=[t_kvag, kb_free[sl]], inc="ls", incn=16)
        tlk = P.tok("ls")
        E("sync", lambda s, kh=kh, j=j, sl=sl: s.dma_start(
            out=s_db[:, sl * 1024:sl * 1024 + nq(j)],
            in_=d_dt[kh:kh + 1, j:j + 1, :, 128 * j:NTOK]),
            waits=[db_free[sl]], inc="ls", incn=16)
        tld = P.tok("ls")
        aslot = abank_n[0] % 2
        abank_n[0] += 1
        b0 = 2 * aslot
        nsg = (nq(j) + 511) // 512
        for sg in range(nsg):
            w = min(512, nq(j) - sg * 512)
            for dc in range(NCH):
                ws2 = []
                if dc == 0:
                    ws2 = [("ls", tlk), bankfree[b0 + sg]]
                    if sg == 0:
                        ws2.append(("vv", t_q_done))
                E("tensor", lambda t, sl=sl, dc=dc, sg=sg, w=w, j=j, b0=b0: t.matmul(
                    bank(b0 + sg, 0, w),
                    s_kb[:, sl * 2048 + dc * 128: sl * 2048 + (dc + 1) * 128],
                    s_U[:, dc * NTOK + 128 * j + sg * 512:
                         dc * NTOK + 128 * j + sg * 512 + w],
                    start=(dc == 0), stop=(dc == NCH - 1)), waits=ws2, inc="mm")
        tmm = P.tok("mm")
        kb_free[sl] = ("mm", tmm)
        E("vector", lambda v, kh=kh, j=j, sl=sl, b0=b0: v.tensor_tensor(
            s_att[:, cb[(kh, j)]:cb[(kh, j)] + nq(j)],
            PS[:, b0 * 512:b0 * 512 + nq(j)],
            s_db[:, sl * 1024:sl * 1024 + nq(j)], OP.mult),
            waits=[("mm", tmm), ("ls", tld)], inc="vv")
        tv = P.tok("vv")
        att_tok[(kh, j)] = tv
        db_free[sl] = ("vv", tv)
        for sg in range(nsg):
            bankfree[b0 + sg] = ("vv", tv)

    def phase_b(i):
        pairs = [(kh, j) for j in range(i + 1) for kh in range(2)]
        stop_tok = {}
        for pi, (kh, j) in enumerate(pairs):
            sl = vb_n[0] % 2
            vb_n[0] += 1
            E("sync", lambda s, kh=kh, j=j, sl=sl: s.dma_start(
                out=s_vb[:, sl * 2048:(sl + 1) * 2048],
                in_=d_kvrb.ap()[kh * 128:(kh + 1) * 128,
                                16384 + j * 2048:16384 + (j + 1) * 2048]),
                waits=[t_kvag, vb_free[sl]], inc="ls", incn=16)
            tlv = P.tok("ls")
            atile = s_att[:, cb[(kh, j)] + 128 * (i - j):
                          cb[(kh, j)] + 128 * (i - j) + 128]
            tok = None
            for n in range(4):
                bk = 4 + n
                ws2 = [("vv", att_tok[(kh, j)])]
                if n == 0:
                    ws2.append(("ls", tlv))
                if pi == 0:
                    ws2.append(bankfree[bk])
                tok = E("tensor", lambda t, atile=atile, sl=sl, n=n, bk=bk,
                        st_=(pi == 0), sp_=(pi == len(pairs) - 1): t.matmul(
                    bank(bk), atile,
                    s_vb[:, sl * 2048 + n * 512: sl * 2048 + (n + 1) * 512],
                    start=st_, stop=sp_),
                    waits=ws2, inc="mm")
                if pi == len(pairs) - 1:
                    stop_tok[bk] = tok
            vb_free[sl] = ("mm", tok)
        E("vector", lambda v: v.tensor_copy(s_S3, PS[:, 2048:4096]),
          waits=[("mm", stop_tok[7]), avac_free[0]], inc="vv")
        tv = P.tok("vv")
        for bk in range(4, 8):
            bankfree[bk] = ("vv", tv)
        aslot = (abank_n[0] + 1) % 2
        b0 = 2 * aslot
        tadd = tv
        for dc in range(NCH):
            bsel = b0 + (dc % 2)
            q = (dc // 2) % 2
            tp = PS[:, bsel * 512 + q * 128: bsel * 512 + (q + 1) * 128]
            ws = [("vv", tv), tp_free[dc % 2]]
            if dc < 2:
                ws += [bankfree[bsel]]
            E("tensor", lambda t, tp=tp, dc=dc: t.transpose(
                tp, s_S3[:, dc * 128:(dc + 1) * 128], s_id[:, :]),
                waits=ws, inc="mm")
            tm = P.tok("mm")
            xsl = s_xf[:, dc * NTOK + i * 128: dc * NTOK + (i + 1) * 128]
            E("vector", lambda v, xsl=xsl, tp=tp: v.tensor_add(xsl, xsl, tp),
              waits=[("mm", tm)], inc="vv")
            tp_free[dc % 2] = ("vv", P.tok("vv"))
            tadd = P.tok("vv")
        bankfree[b0] = bankfree[b0 + 1] = ("vv", tadd)
        avac_free[0] = ("vv", tadd)
        return tadd

    t_attn = 0
    for j in range(8):
        phase_a(0, j)
        phase_a(1, j)
        t_attn = phase_b(j)

    if "kvrb" in tap_ap:
        E("gpsimd", lambda g: g.dma_start(out=tap_ap["kvrb"], in_=d_kvrb.ap()),
          waits=[t_kvag], inc="lg", incn=16)
    if "att" in tap_ap:
        E("gpsimd", lambda g: g.dma_start(out=tap_ap["att"], in_=s_att[:, :]),
          waits=[("vv", t_attn)], inc="lg", incn=16)
    if "x2" in tap_ap:
        E("gpsimd", lambda g: g.dma_start(out=tap_ap["x2"], in_=s_xf[:, :]),
          waits=[("vv", t_attn)], inc="lg", incn=16)

    if cut == "attn":
        _finish_early(); _realize(); return nc

    for tt in range(NT):
        tokl = ln_tile(tt, extra_waits=[("vv", t_attn)])
        ffn(1, tt, tokl)
    n_st = P.tok("sa")
    E("scalar", lambda a: a.nop(), waits=[("sa", n_st)])
    _realize()
    return nc


_NC_CACHE = {}


def _get_nc(taps=()):
    key = (tuple(taps), _CUT, _QWIN)
    if key not in _NC_CACHE:
        _NC_CACHE[key] = _build_nc(taps, _CUT, _QWIN)
    return _NC_CACHE[key]


def _wqk_blocks(W):
    return np.ascontiguousarray(
        W.reshape(16, 128, 16, 128).transpose(2, 1, 0, 3).reshape(H, H))


def _wv_blocks(W):
    return np.ascontiguousarray(
        W.reshape(16, 128, 2, 1024).transpose(2, 0, 1, 3).reshape(2 * H, H // 2))


def _tbl_blocks(t):
    t = np.ascontiguousarray(t, np.float32)
    return np.ascontiguousarray(
        t.reshape(16, 128, 2, TT).transpose(0, 2, 1, 3).reshape(2 * H, TT))


def _make_in_maps(L0_attn_X, L0_attn_W_Q, L0_attn_W_K, L0_attn_W_V,
                  L0_ffn_w1, L0_ffn_w2, L0_ffn_w3):
    X = np.asarray(L0_attn_X, dtype=np.float32)
    WQ = np.ascontiguousarray(np.asarray(L0_attn_W_Q, dtype=np.float32))
    WK = np.ascontiguousarray(np.asarray(L0_attn_W_K, dtype=np.float32))
    WV = np.ascontiguousarray(np.asarray(L0_attn_W_V, dtype=np.float32))
    w1T = np.ascontiguousarray(np.asarray(L0_ffn_w1, np.float32).T)
    w2T = np.ascontiguousarray(np.asarray(L0_ffn_w2, np.float32).T)
    w3T = np.ascontiguousarray(np.asarray(L0_ffn_w3, np.float32).T)
    cqT, sqT, ckT, skT, D = _host_tables()
    ident = np.eye(128, dtype=np.float32)

    w_fl = [w1T.reshape(8, 128, WSL), w2T.reshape(8, 128, WSL),
            w3T.reshape(8, 128, WSL)]

    in_maps = []
    for c in range(8):
        b, par = c // 2, c % 2
        xT = np.ascontiguousarray(X[b, par::2, :].T)
        Dq = D[par::2, :]
        dtt = np.empty((2, 8, 128, NTOK), np.float32)
        for kh in range(2):
            dtt[kh] = np.ascontiguousarray(Dq[:, kh::2].T).reshape(8, 128, NTOK)
        m = {
            "xT": xT,
            "w0sl": np.ascontiguousarray(w_fl[0][c]),
            "w1sl": np.ascontiguousarray(w_fl[1][c]),
            "w2sl": np.ascontiguousarray(w_fl[2][c]),
            "WQ": _wqk_blocks(WQ), "WK": _wqk_blocks(WK),
            "WV": _wv_blocks(WV),
            "cq": _tbl_blocks(cqT[:, par::2]),
            "sq": _tbl_blocks(sqT[:, par::2]),
            "ck": _tbl_blocks(ckT[:, par::2]),
            "sk": _tbl_blocks(skT[:, par::2]),
            "dtt": dtt,
            "ident": ident,
            "kv_sel": np.array([[256 * b]], np.int32),
        }
        if _QWIN:
            import ml_dtypes
            for wi, wT in enumerate((w1T, w2T, w3T)):
                g = np.abs(wT).mean(dtype=np.float64)
                q = np.clip(np.round(wT / (np.float32(g) + np.float32(EPS_Q))),
                            -1, 1).astype(ml_dtypes.bfloat16)
                m[f"q{wi}win"] = q
        in_maps.append(m)
    return in_maps


def kernel(L0_attn_X, L0_attn_W_Q, L0_attn_W_K, L0_attn_W_V,
           L0_ffn_w1, L0_ffn_w2, L0_ffn_w3):
    in_maps = _make_in_maps(L0_attn_X, L0_attn_W_Q, L0_attn_W_K, L0_attn_W_V,
                            L0_ffn_w1, L0_ffn_w2, L0_ffn_w3)
    nc = _get_nc(_TAPS)
    res = run_bass_kernel_spmd(nc, in_maps, core_ids=list(range(8)))

    out = np.empty((B, S, H), np.float32)
    for c in range(8):
        b, par = c // 2, c % 2
        out[b, par::2, :] = res.results[c]["y"].T
    if _TAPS:
        kernel._tap_results = res.results
    return out


# version 04 (final)
# speedup vs baseline: 1.0444x; 1.0340x over previous
import numpy as np
from collections import defaultdict

import concourse.bass as bass
import concourse.mybir as mybir
from concourse.bass_core_rust import IndirectOffsetOnAxis
from concourse.expressions import RuntimeValue
from concourse.bass_utils import run_bass_kernel_spmd

F32 = mybir.dt.float32
BF16 = mybir.dt.bfloat16
OP = mybir.AluOpType
AF = mybir.ActivationFunctionType
AX = mybir.AxisListType

B, S, H, HF = 4, 2048, 2048, 5632
NCH = H // 128
HC = HF // 128
NTOK = 1024
TT = 512
NT = NTOK // TT
WSL = (H * HF) // (8 * 128)
NW = H * HF
EPS_Q = 1e-8
EPS_LN = 1e-5

_TAPS = ()
_CUT = None
_QWIN = False

SWAP32 = [i ^ 1 for i in range(32)]
ENGS = ("sync", "scalar", "tensor", "vector", "gpsimd")


def _host_tables():
    def _mk(xp):
        f32 = xp.float32
        d = H
        half = d // 2
        scale_vec = (xp.arange(0, d, 2, dtype=f32) + f32(0.4 * d)) / f32(1.4 * d)
        power = xp.arange(S, dtype=f32) / f32(512.0)
        scale = scale_vec[None, :] ** power[:, None]
        inv_freq = f32(1.0) / (f32(10000.0) ** (xp.arange(half, dtype=f32) / f32(half)))
        sinusoid = xp.arange(S, dtype=f32)[:, None] * inv_freq[None, :]
        sin = xp.sin(sinusoid)
        cos = xp.cos(sinusoid)
        scale_dn = f32(1.0) / scale

        def rep2(a):
            return xp.repeat(a, 2, axis=-1)
        sin_q = rep2(sin * scale)
        cos_q = rep2(cos * scale)
        sin_k = rep2(sin * scale_dn)
        cos_k = rep2(cos * scale_dn)
        n = xp.arange(S)[:, None]
        m = xp.arange(S)[None, :]
        dm = xp.maximum(n - m, 0).astype(f32)
        D = xp.where(n >= m, f32(0.96875) ** dm, f32(0.0))
        return sin_q, cos_q, sin_k, cos_k, D

    try:
        import jax
        import jax.numpy as jnp
        cpu = jax.devices("cpu")[0]
        with jax.default_device(cpu):
            outs = tuple(np.asarray(o) for o in _mk(jnp))
    except Exception:
        outs = tuple(np.asarray(o) for o in _mk(np))
    sin_q, cos_q, sin_k, cos_k, D = outs
    sgn = np.ones((H,), np.float32)
    sgn[0::2] = -1.0
    sin_q = sin_q * sgn[None, :]
    sin_k = sin_k * sgn[None, :]
    return (np.ascontiguousarray(cos_q.T), np.ascontiguousarray(sin_q.T),
            np.ascontiguousarray(cos_k.T), np.ascontiguousarray(sin_k.T), D)


class Plan:

    def __init__(self, nc):
        self.nc = nc
        self.cnt = defaultdict(int)
        self.ops = {e: [] for e in ENGS}
        self.sems = {}

    def sem(self, name):
        if name not in self.sems:
            self.sems[name] = self.nc.alloc_semaphore(name=name)
        return self.sems[name]

    def emit(self, eng, fn, waits=(), inc=None, incn=1):
        ws = [w for w in waits if w is not None and w[1] and w[1] > 0]
        incs = []
        tok = None
        if inc is not None:
            self.cnt[inc] += incn
            tok = self.cnt[inc]
            incs.append((inc, incn))
        self.ops[eng].append((ws, fn, incs))
        return tok

    def tok(self, sem):
        return self.cnt[sem]

    def realize(self, eng_name, eng):
        lastw = {}
        for ws, fn, incs in self.ops[eng_name]:
            for s, v in ws:
                if lastw.get(s, -1) >= v:
                    continue
                lastw[s] = v
                eng.wait_ge(self.sem(s), v)
            inst = fn(eng)
            for s, n in incs:
                inst.then_inc(self.sem(s), n)


def _build_nc(taps=(), cut=None, qwin=False):
    nc = bass.Bass("TRN2", target_bir_lowering=False, detect_race_conditions=False)

    d_xT = nc.dram_tensor("xT", [H, NTOK], F32, kind="ExternalInput").ap()
    d_wsl = [nc.dram_tensor(f"w{i}sl", [128, WSL], F32, kind="ExternalInput").ap()
             for i in range(3)]
    d_WQ = nc.dram_tensor("WQ", [H, H], F32, kind="ExternalInput").ap()
    d_WK = nc.dram_tensor("WK", [H, H], F32, kind="ExternalInput").ap()
    d_WV = nc.dram_tensor("WV", [2 * H, H // 2], F32, kind="ExternalInput").ap()
    d_cq = nc.dram_tensor("cq", [2 * H, TT], F32, kind="ExternalInput").ap()
    d_sq = nc.dram_tensor("sq", [2 * H, TT], F32, kind="ExternalInput").ap()
    d_ck = nc.dram_tensor("ck", [2 * H, TT], F32, kind="ExternalInput").ap()
    d_sk = nc.dram_tensor("sk", [2 * H, TT], F32, kind="ExternalInput").ap()
    d_dt = nc.dram_tensor("dtt", [2, 8, 128, NTOK], F32, kind="ExternalInput").ap()
    d_id = nc.dram_tensor("ident", [128, 128], F32, kind="ExternalInput").ap()
    d_y = nc.dram_tensor("y", [H, NTOK], F32, kind="ExternalOutput").ap()

    d_gin = nc.dram_tensor("g_in", [1, 4], F32)
    d_gout = nc.dram_tensor("g_out", [1, 4], F32)
    d_bin = nc.dram_tensor("b_in", [1, 4], F32)
    d_bout = nc.dram_tensor("b_out", [1, 4], F32)
    d_bin2 = nc.dram_tensor("b_in2", [1, 4], F32)
    d_bout2 = nc.dram_tensor("b_out2", [1, 4], F32)
    d_qin = [nc.dram_tensor(f"q{i}in", [128, WSL], BF16) for i in range(3)]
    d_qw = [nc.dram_tensor("q0w", [H, HF], BF16),
            nc.dram_tensor("q1w", [H, HF], BF16),
            nc.dram_tensor("q2w", [HF, H], BF16)]
    d_qwb = [nc.dram_tensor("q0wb", [H, HF], BF16),
             nc.dram_tensor("q1wb", [H, HF], BF16),
             nc.dram_tensor("q2wb", [HF, H], BF16)]
    d_qwc = d_qwb
    if qwin:
        d_qwc = [nc.dram_tensor("q0win", [H, HF], BF16, kind="ExternalInput"),
                 nc.dram_tensor("q1win", [H, HF], BF16, kind="ExternalInput"),
                 nc.dram_tensor("q2win", [HF, H], BF16, kind="ExternalInput")]
    d_kvs = nc.dram_tensor("kv_s", [128, 32768], BF16)
    d_kvag = nc.dram_tensor("kv_ag", [1024, 32768], BF16)
    d_kvrb = nc.dram_tensor("kv_rb", [256, 32768], BF16)
    d_kvsel = nc.dram_tensor("kv_sel", [1, 1], mybir.dt.int32, kind="ExternalInput").ap()

    tap_ap = {}
    for t in list(taps) + (["thr_gr", "thr_qred"] if "thr" in taps else []):
        shp, dt = {"xln": ([128, NCH * TT], BF16), "qf": ([128, NCH * NTOK], BF16),
                   "x1": ([128, NCH * NTOK], F32), "x2": ([128, NCH * NTOK], F32),
                   "att": ([128, 9216], BF16), "kvs": ([128, 32768], BF16),
                   "hid": ([128, HC * TT], BF16), "qw0": ([H, HF], BF16), "kvrb": ([256, 32768], BF16),
                   "thr": ([128, 8], F32), "thr_gr": ([1, 16], F32), "lnmr": ([128, 1024], F32), "qw0s": ([128, 1024], BF16), "k1dbg": ([128, 3072], BF16),
                   "thr_qred": ([128, 40], F32)}[t]
        tap_ap[t] = nc.dram_tensor(f"tap_{t}", shp, dt, kind="ExternalOutput").ap()

    A = nc.alloc_sbuf_tensor
    s_xf = A("xf", [128, NCH * NTOK], F32)
    s_xln = A("xlnb", [128, NCH * TT], BF16)
    s_U = A("U", [128, 32768], BF16)
    s_att = A("attsb", [128, 9216], BF16)
    s_wr = A("wring", [128, 8 * TT], BF16)
    s_S1 = A("S1", [128, 4096], BF16)
    s_S2 = A("S2", [128, 3072], F32)
    s_id = A("idsb", [128, 128], F32)
    s_on = A("onescol", [128, 4], F32)
    s_onb = A("onescolb", [128, 1], BF16)
    s_orow = A("onesrow", [1, 128], F32)
    s_thr = A("thr", [128, 8], F32)
    s_gr = A("grow", [1, 16], F32)
    s_kvix = A("kvix", [1, 1], mybir.dt.int32)

    U32 = s_U.ap().bitcast(F32)
    v_qbuf = [U32[:, 0:1408], U32[:, 1408:2816]]
    v_qout = s_U[:, 8192:8192 + WSL]
    S2b = s_S2.ap().bitcast(BF16)
    v_kev = [S2b[:, 0:512], S2b[:, 512:1024]]
    v_kev2 = [S2b[:, 1024:1536], S2b[:, 1536:2048]]
    v_vev = [S2b[:, 2048:3072], S2b[:, 3072:4096]]
    v_kout = [S2b[:, 4096:4608], S2b[:, 4608:5120]]
    v_mrbc = s_S2[:, 0:1024]
    v_m = s_S2[0:1, 1024:1536]
    v_q2 = s_S2[0:1, 1536:2048]
    v_ac = s_S2[0:1, 2048:3072]
    v_sg = s_S1[:, 0:2048]
    v_xbsc = s_S1[:, 2048:4096]
    v_xscr = v_xbsc.bitcast(F32)
    v_qredv = s_S2[:, 0:24]
    s_S3 = s_S2[:, 0:2048]
    s_db = s_U.ap().bitcast(F32)[:, 8192:10240]
    s_vb = s_U[:, 20480:24576]
    s_kb = s_U[:, 24576:28672]
    v_qredw = s_S2[:, 32:35]
    v_qn = s_S1[:, 0:1408]
    v_wqb = [s_S1[:, 0:2048], s_S1[:, 2048:4096]]
    v_wqf = [U32[:, 8192:10240], U32[:, 10240:12288]]
    attF = s_att.ap().bitcast(F32)
    v_tbf = [attF[:, 0:1024], attF[:, 1024:2048]]

    PS = nc.alloc_psum_tensor("PS", [128, 4096], F32)

    def bank(i, n0=0, w=512):
        return PS[:, i * 512 + n0:i * 512 + n0 + w]

    P = Plan(nc)
    E = P.emit
    for s in ("ls", "lg", "sa", "cc", "mm", "vv", "aa"):
        P.sem(s)

    bankfree = {i: None for i in range(8)}

    E("vector", lambda v: v.memset(s_on[:, :], 1.0))
    E("vector", lambda v: v.memset(s_onb[:, :], 1.0))
    E("vector", lambda v: v.memset(s_orow[:, :], 1.0))
    E("vector", lambda v: v.memset(s_gr[:, :], 0.0), inc="vv")
    E("gpsimd", lambda g: g.dma_start(out=s_id[:, :], in_=d_id[:, :]), inc="lg", incn=16)
    for dc in range(NCH):
        E("gpsimd", lambda g, dc=dc: g.dma_start(
            out=s_xf[:, dc * NTOK:(dc + 1) * NTOK],
            in_=d_xT[dc * 128:(dc + 1) * 128, :]), inc="lg", incn=16)
    t_xload = P.tok("lg")

    NQT = 8
    QW = WSL // NQT
    red_tok = []
    for wi in range(3):
        for k in range(NQT):
            i = wi * NQT + k
            sl = i % 2
            wprev = ("vv", red_tok[i - 2]) if i >= 2 else None
            E("sync", lambda s, wi=wi, k=k, sl=sl: s.dma_start(
                out=v_qbuf[sl], in_=d_wsl[wi][:, k * QW:(k + 1) * QW]),
                waits=[wprev], inc="ls", incn=16)
            tl = P.tok("ls")
            E("vector", lambda v, i=i, sl=sl: v.tensor_reduce(
                v_qredv[:, i:i + 1], v_qbuf[sl], AX.X, OP.add, True),
                waits=[("ls", tl)], inc="vv")
            red_tok.append(P.tok("vv"))
    for wi in range(3):
        E("vector", lambda v, wi=wi: v.tensor_reduce(
            v_qredw[:, wi:wi + 1], v_qredv[:, wi * NQT:(wi + 1) * NQT],
            AX.X, OP.add, False), inc="vv")
    t_qred = P.tok("vv")
    E("tensor", lambda t: t.matmul(PS[0:1, 3072:3075], s_on[:, 0:1], v_qredw[:, 0:3],
                                   start=True, stop=True),
      waits=[("vv", t_qred)], inc="mm")
    t_gmm = P.tok("mm")
    E("vector", lambda v: v.tensor_copy(s_gr[0:1, 0:3], PS[0:1, 3072:3075]),
      waits=[("mm", t_gmm)], inc="vv")
    t_gcp = P.tok("vv")
    E("gpsimd", lambda g: g.dma_start(out=d_gin.ap()[0:1, 0:4], in_=s_gr[0:1, 0:4]),
      waits=[("vv", t_gcp)], inc="lg", incn=16)
    t_gw = P.tok("lg")
    E("gpsimd", lambda g: g.collective_compute(
        "AllReduce", OP.add, replica_groups=[list(range(8))],
        ins=[d_gin.ap().opt()], outs=[d_gout.ap().opt()]),
        waits=[("lg", t_gw)], inc="cc")
    t_ar = P.tok("cc")
    E("gpsimd", lambda g: g.dma_start(out=s_gr[0:1, 4:8], in_=d_gout.ap()[0:1, 0:4]),
      waits=[("cc", t_ar)], inc="lg", incn=16)
    t_grl = P.tok("lg")
    c1 = float(np.float32(0.5) / np.float32(NW))
    c2 = float(0.5 * EPS_Q)
    E("tensor", lambda t: t.matmul(PS[:, 3072:3075], s_orow[0:1, :], s_gr[0:1, 4:7],
                                   start=True, stop=True),
      waits=[("lg", t_grl)], inc="mm")
    t_tbc = P.tok("mm")
    E("vector", lambda v: v.tensor_scalar(s_thr[:, 0:3], PS[:, 3072:3075], c1, c2,
                                          OP.mult, OP.add),
      waits=[("mm", t_tbc)])
    E("vector", lambda v: v.tensor_scalar(s_thr[:, 3:6], PS[:, 3072:3075], -c1, -c2,
                                          OP.mult, OP.add), inc="vv")
    t_thr = P.tok("vv")
    bankfree[6] = bankfree[7] = ("vv", t_thr)

    t_qag = []
    qv_tok = []
    qw_tok = {}
    for wi in range(3):
        t_last = 0
        for k in range(NQT):
            i = wi * NQT + k
            sl = i % 2
            wprev = ("vv", qv_tok[i - 2]) if i >= 2 else ("vv", t_thr)
            E("sync", lambda s, wi=wi, k=k, sl=sl: s.dma_start(
                out=v_qbuf[sl], in_=d_wsl[wi][:, k * QW:(k + 1) * QW]),
                waits=[wprev, ("cc", t_ar)], inc="ls", incn=16)
            tl = P.tok("ls")
            qp = v_qout[:, k * QW:(k + 1) * QW]
            wq = ("lg", qw_tok[(wi - 1, k)]) if wi > 0 else None
            E("vector", lambda v, qp=qp, sl=sl, wi=wi: v.tensor_scalar(
                qp, v_qbuf[sl], s_thr[:, wi:wi + 1], None, OP.is_gt),
                waits=[("ls", tl), ("vv", t_thr), wq])
            E("vector", lambda v, sl=sl, wi=wi: v.tensor_scalar(
                v_qn, v_qbuf[sl], s_thr[:, 3 + wi:4 + wi], None, OP.is_lt))
            E("vector", lambda v, qp=qp: v.tensor_sub(qp, qp, v_qn), inc="vv")
            qv_tok.append(P.tok("vv"))
            E("gpsimd", lambda g, wi=wi, k=k, qp=qp: g.dma_start(
                out=d_qin[wi][:, k * QW:(k + 1) * QW], in_=qp),
                waits=[("vv", qv_tok[-1])], inc="lg", incn=16)
            t_last = P.tok("lg")
            qw_tok[(wi, k)] = t_last
    for wi in range(3):
        E("gpsimd", lambda g, wi=wi: g.collective_compute(
            "AllGather", OP.bypass, replica_groups=[list(range(8))],
            ins=[d_qin[wi].ap().opt()], outs=[d_qw[wi].ap().opt()]),
            waits=[("lg", t_last)], inc="cc")
    E("gpsimd", lambda g: g.collective_compute(
        "AllReduce", OP.add, replica_groups=[list(range(8))],
        ins=[d_bin.ap().opt()], outs=[d_bout.ap().opt()]), inc="cc")
    t_bar = P.tok("cc")
    snapA = [[None] * NCH, [None] * NCH]
    snapB = [None] * HC
    for dc in range(NCH):
        for wi in range(2):
            E("gpsimd", lambda g, wi=wi, dc=dc: g.dma_start(
                out=d_qwb[wi].ap()[dc * 128:(dc + 1) * 128, :],
                in_=d_qw[wi].ap()[dc * 128:(dc + 1) * 128, :]),
                waits=[("cc", t_bar)], inc="lg", incn=16)
            snapA[wi][dc] = ("lg", P.tok("lg"))
    for hc in range(HC):
        E("gpsimd", lambda g, hc=hc: g.dma_start(
            out=d_qwb[2].ap()[hc * 128:(hc + 1) * 128, :],
            in_=d_qw[2].ap()[hc * 128:(hc + 1) * 128, :]),
            waits=[("cc", t_bar)], inc="lg", incn=16)
        snapB[hc] = ("lg", P.tok("lg"))
    t_qag.extend([snapA[0][NCH - 1], snapA[1][NCH - 1], snapB[HC - 1]])
    if "thr" in tap_ap:
        E("gpsimd", lambda g: g.dma_start(out=tap_ap["thr"], in_=s_thr[:, :]),
          waits=[("vv", t_thr)], inc="lg", incn=16)
        E("gpsimd", lambda g: g.dma_start(out=tap_ap["thr_gr"], in_=s_gr[:, :]),
          waits=[("vv", t_thr)], inc="lg", incn=16)
        E("gpsimd", lambda g: g.dma_start(out=tap_ap["thr_qred"][:, 0:35], in_=s_S2[:, 0:35]),
          waits=[("vv", t_thr)], inc="lg", incn=16)
    if "qw0" in tap_ap:
        E("sync", lambda s: s.dma_start(out=tap_ap["qw0"], in_=d_qw[0].ap()),
          waits=[t_qag[2]], inc="ls", incn=16)


    lnslot = [0, 0]

    def ln_tile(tt, extra_waits=()):
        fence = [("mm", P.tok("mm")), ("lg", P.tok("lg")), ("aa", P.tok("aa"))]
        for dc in range(NCH):
            sl = dc % 2
            xsl = s_xf[:, dc * NTOK + tt * TT: dc * NTOK + tt * TT + TT]
            xb = v_xbsc[:, sl * 512:sl * 512 + 512]
            xq = v_xbsc[:, 1024 + sl * 512:1024 + sl * 512 + 512]
            ws = (list(extra_waits) + fence) if dc == 0 else [("mm", lnslot[sl])]
            E("vector", lambda v, xb=xb, xsl=xsl: v.tensor_copy(xb, xsl), waits=ws)
            E("vector", lambda v, xb=xb, xq=xq: v.tensor_mul(xq, xb, xb), inc="vv")
            tv = P.tok("vv")
            ws2 = [("vv", tv)]
            if dc == 0:
                ws2 += [bankfree[6], bankfree[7]]
            E("tensor", lambda t, xb=xb, dc=dc: t.matmul(
                PS[0:1, 3072:3584], s_onb[:, 0:1], xb,
                start=(dc == 0), stop=(dc == NCH - 1)), waits=ws2)
            E("tensor", lambda t, xq=xq, dc=dc: t.matmul(
                PS[0:1, 3584:4096], s_onb[:, 0:1], xq,
                start=(dc == 0), stop=(dc == NCH - 1)), inc="mm")
            lnslot[sl] = P.tok("mm")
        t_st = P.tok("mm")
        cN = float(np.float32(1.0) / np.float32(H))
        E("vector", lambda v: v.tensor_copy(v_m, PS[0:1, 3072:3584]),
          waits=[("mm", t_st)])
        E("vector", lambda v: v.tensor_copy(v_q2, PS[0:1, 3584:4096]), inc="vv")
        tv = P.tok("vv")
        E("tensor", lambda t: t.matmul(PS[:, 3072:3584], s_orow[0:1, :], v_m,
                                       start=True, stop=True), waits=[("vv", tv)])
        E("tensor", lambda t: t.matmul(PS[:, 3584:4096], s_orow[0:1, :], v_q2,
                                       start=True, stop=True), inc="mm")
        tm = P.tok("mm")
        sA = v_xscr[:, 0:512]
        sB = v_xscr[:, 512:1024]
        E("vector", lambda v: v.tensor_scalar(v_mrbc[:, 512:1024], PS[:, 3072:3584],
                                              cN, None, OP.mult), waits=[("mm", tm)])
        E("vector", lambda v: v.tensor_scalar(sA, PS[:, 3584:4096], cN, None, OP.mult),
          inc="vv")
        t1_ = P.tok("vv")
        bankfree[6] = bankfree[7] = ("vv", t1_)
        E("vector", lambda v: v.tensor_mul(sB, v_mrbc[:, 512:1024],
                                           v_mrbc[:, 512:1024]),
          waits=[("vv", t1_)], inc="vv")
        t2_ = P.tok("vv")
        E("vector", lambda v: v.tensor_sub(sA, sA, sB), waits=[("vv", t2_)], inc="vv")
        t3_ = P.tok("vv")
        E("vector", lambda v: v.tensor_scalar(sA, sA, EPS_LN, None, OP.add),
          waits=[("vv", t3_)], inc="vv")
        t4_ = P.tok("vv")
        E("scalar", lambda a: a.activation(sB, sA, AF.Ln), waits=[("vv", t4_)],
          inc="aa")
        t5_ = P.tok("aa")
        E("scalar", lambda a: a.activation(v_mrbc[:, 0:512], sB, AF.Exp, scale=-0.5),
          waits=[("aa", t5_)], inc="aa")
        t_exp = P.tok("aa")
        for dc in range(NCH):
            sl = dc % 2
            xsl = s_xf[:, dc * NTOK + tt * TT: dc * NTOK + tt * TT + TT]
            u = v_xscr[:, sl * 512:sl * 512 + 512]
            ws3 = [("aa", t_exp)] if dc == 0 else []
            E("vector", lambda v, u=u, xsl=xsl: v.tensor_tensor(
                u, xsl, v_mrbc[:, 512:1024], OP.subtract), waits=ws3)
            E("vector", lambda v, u=u, dc=dc: v.tensor_tensor(
                s_xln[:, dc * TT:(dc + 1) * TT], u, v_mrbc[:, 0:512], OP.mult),
                inc="vv")
        return P.tok("vv")

    wr_n = [0]
    wr_free = [None] * 8

    def wr_load(src_fn, extra=None):
        slot = wr_n[0] % 8
        E("sync", lambda s, slot=slot: s.dma_start(
            out=s_wr[:, slot * TT:(slot + 1) * TT], in_=src_fn()),
            waits=[wr_free[slot], extra], inc="ls", incn=16)
        wr_n[0] += 1
        return slot, P.tok("ls")

    def ffn(app, tt, xln_tok):
        for g in range(11):
            stop_tok = {}
            for which in (0, 1):
                base = 0 if which == 0 else 4
                for dc in range(NCH):
                    slot, tl = wr_load(lambda wi=which, g=g, dc=dc: d_qwc[wi].ap()[
                        dc * 128:(dc + 1) * 128, g * 512:(g + 1) * 512],
                        extra=snapA[which][dc])
                    tok = None
                    for k in range(4):
                        bk = base + k
                        ws = []
                        if k == 0:
                            ws.append(("ls", tl))
                        if dc == 0:
                            ws.append(bankfree[bk])
                            if k == 0:
                                ws.append(("vv", xln_tok))
                        tok = E("tensor", lambda t, slot=slot, k=k, dc=dc, bk=bk:
                                t.matmul(
                            bank(bk),
                            s_wr[:, slot * TT + k * 128: slot * TT + (k + 1) * 128],
                            s_xln[:, dc * TT:(dc + 1) * TT],
                            start=(dc == 0), stop=(dc == NCH - 1)),
                            waits=ws, inc="mm")
                        if dc == NCH - 1:
                            stop_tok[bk] = tok
                    wr_free[slot] = ("mm", tok)
            for k in range(4):
                E("scalar", lambda a, k=k: a.activation(
                    v_sg[:, k * 512:(k + 1) * 512], bank(k), AF.Sigmoid),
                    waits=[("mm", stop_tok[k])], inc="aa")
                bankfree[k] = ("aa", P.tok("aa"))
            ta = P.tok("aa")
            for k in range(4):
                hc = 4 * g + k
                E("vector", lambda v, k=k, hc=hc: v.tensor_tensor(
                    s_U[:, hc * TT:(hc + 1) * TT], v_sg[:, k * 512:(k + 1) * 512],
                    bank(4 + k), OP.mult),
                    waits=[("aa", ta), ("mm", stop_tok[4 + k])], inc="vv")
                bankfree[4 + k] = ("vv", P.tok("vv"))
        t_hid = P.tok("vv")
        if "hid" in tap_ap and app == 0 and tt == 0:
            E("gpsimd", lambda g: g.dma_start(out=tap_ap["hid"], in_=s_U[:, 0:HC * TT]),
              waits=[("vv", t_hid)], inc="lg", incn=16)
        for dg in range(4):
            stop_tok = {}
            for hc in range(HC):
                slot, tl = wr_load(lambda dg=dg, hc=hc: d_qwc[2].ap()[
                    hc * 128:(hc + 1) * 128, dg * 512:(dg + 1) * 512],
                    extra=snapB[hc])
                tok = None
                for k in range(4):
                    ws = []
                    if k == 0:
                        ws.append(("ls", tl))
                    if hc == 0:
                        ws.append(bankfree[k])
                        if k == 0 and dg == 0:
                            ws.append(("vv", t_hid))
                    tok = E("tensor", lambda t, slot=slot, k=k, hc=hc: t.matmul(
                        bank(k),
                        s_wr[:, slot * TT + k * 128: slot * TT + (k + 1) * 128],
                        s_U[:, hc * TT:(hc + 1) * TT],
                        start=(hc == 0), stop=(hc == HC - 1)),
                        waits=ws, inc="mm")
                    if hc == HC - 1:
                        stop_tok[k] = tok
                wr_free[slot] = ("mm", tok)
            for k in range(4):
                dc = 4 * dg + k
                xsl = s_xf[:, dc * NTOK + tt * TT: dc * NTOK + tt * TT + TT]
                E("vector", lambda v, k=k, xsl=xsl: v.scalar_tensor_tensor(
                    xsl, bank(k), 0.5, xsl, OP.mult, OP.add),
                    waits=[("mm", stop_tok[k])], inc="vv")
                bankfree[k] = ("vv", P.tok("vv"))
                if app == 1:
                    tv = P.tok("vv")
                    E("scalar", lambda a, dc=dc, xsl=xsl: a.dma_start(
                        out=d_y[dc * 128:(dc + 1) * 128, tt * TT:(tt + 1) * TT],
                        in_=xsl), waits=[("vv", tv)], inc="sa", incn=16)
        return P.tok("vv")

    def _finish_early():
        tv = P.tok("vv")
        ta = P.tok("aa")
        tm = P.tok("mm")
        for dc in range(NCH):
            E("scalar", lambda a, dc=dc: a.dma_start(
                out=d_y[dc * 128:(dc + 1) * 128, :],
                in_=s_xf[:, dc * NTOK:(dc + 1) * NTOK]),
                waits=[("vv", tv), ("aa", ta), ("mm", tm), ("lg", P.tok("lg")),
                       ("ls", P.tok("ls")), ("cc", P.tok("cc"))],
                inc="sa", incn=16)
        E("scalar", lambda a: a.nop(), waits=[("sa", P.tok("sa"))])

    def _realize():
        with nc.Block() as block:
            @block.sync
            def _(e):
                P.realize("sync", e)

            @block.scalar
            def _(e):
                P.realize("scalar", e)

            @block.tensor
            def _(e):
                P.realize("tensor", e)

            @block.vector
            def _(e):
                P.realize("vector", e)

            @block.gpsimd
            def _(e):
                P.realize("gpsimd", e)

    if cut == "quant":
        _finish_early(); _realize(); return nc

    tok = ln_tile(0, extra_waits=[("lg", t_xload)])
    tok_f0 = ffn(0, 0, tok)
    tok = ln_tile(1)
    if "lnmr" in tap_ap:
        E("gpsimd", lambda g: g.dma_start(out=tap_ap["lnmr"], in_=v_mrbc[:, 0:1024]),
          waits=[("vv", tok)], inc="lg", incn=16)
    if "xln" in tap_ap and cut == "ffn1":
        E("gpsimd", lambda g: g.dma_start(out=tap_ap["xln"], in_=s_xln[:, :]),
          waits=[("vv", tok)], inc="lg", incn=16)
    tok_f1 = ffn(0, 1, tok)
    t_ffn1 = max(tok_f0, tok_f1)

    if "x1" in tap_ap:
        E("gpsimd", lambda g: g.dma_start(out=tap_ap["x1"], in_=s_xf[:, :]),
          waits=[("vv", t_ffn1)], inc="lg", incn=16)

    if cut == "ffn1":
        _finish_early(); _realize(); return nc

    wq_free = [None, None]
    tb_free = [None, None]
    kout_free = [None, None]
    vev_free = [None, None]
    kvg_wb = [None, None]
    pbank = [0]
    proj_last = [0]

    def _kvs_k_dst(jc, tt):
        base = tt * 4 * 2048 + jc * 128
        ap = d_kvs.ap()
        flat = ap[:, base:base + 3 * 2048 + 128]
        return bass.AP(flat.tensor, flat.offset,
                       [list(flat.ap[0]), [2048, 4], [1, 128]])

    def proj_qk(which, tt, xln_tok):
        dW = d_WQ if which == "q" else d_WK
        dcos = d_cq if which == "q" else d_ck
        dsin = d_sq if which == "q" else d_sk
        def _wload(jc):
            E("gpsimd", lambda g, jc=jc: g.dma_start(
                out=v_wqf[jc % 2], in_=dW[jc * 128:(jc + 1) * 128, :]),
                waits=[wq_free[jc % 2], ("vv", t_ffn1)], inc="lg", incn=16)
            return P.tok("lg")
        _wload(0)
        for jc in range(NCH):
            sl = jc % 2
            wsl_ = v_wqb[sl]
            wfs = v_wqf[sl]
            tlwf = _wload(jc + 1) if jc + 1 < NCH else P.tok("lg")
            E("vector", lambda v, wsl_=wsl_, wfs=wfs: v.scalar_tensor_tensor(
                wsl_, wfs, 1.0, wfs, OP.mult, OP.max),
                waits=[("lg", tlwf), wq_free[sl]], inc="vv")
            tlw = ("vv", P.tok("vv"))
            ctb = v_tbf[sl][:, 0:512]
            stb = v_tbf[sl][:, 512:1024]
            blk = jc * 2 + tt
            E("scalar", lambda a, ctb=ctb, dcos=dcos, blk=blk: a.dma_start(
                out=ctb, in_=dcos[blk * 128:(blk + 1) * 128, :]),
                waits=[tb_free[sl]], inc="sa", incn=16)
            E("scalar", lambda a, stb=stb, dsin=dsin, blk=blk: a.dma_start(
                out=stb, in_=dsin[blk * 128:(blk + 1) * 128, :]),
                inc="sa", incn=16)
            tlt = P.tok("sa")
            bk = pbank[0] % 4
            pbank[0] += 1
            ws0 = [tlw, bankfree[bk]]
            if jc == 0:
                ws0.append(("vv", xln_tok))
            for dc in range(NCH):
                E("tensor", lambda t, wsl_=wsl_, dc=dc, bk=bk: t.matmul(
                    bank(bk), wsl_[:, dc * 128:(dc + 1) * 128],
                    s_xln[:, dc * TT:(dc + 1) * TT],
                    start=(dc == 0), stop=(dc == NCH - 1)),
                    waits=(ws0 if dc == 0 else []), inc="mm")
            tmm = P.tok("mm")
            wq_free[sl] = ("mm", tmm)
            kb, rot = v_kev[0], v_kev[1]
            tb_, ub = v_kev2[0], v_kev2[1]
            if which == "q":
                out = s_U[:, jc * NTOK + tt * TT: jc * NTOK + tt * TT + TT]
            else:
                out = v_kout[1 - sl]
            ws1 = [("mm", tmm), ("sa", tlt), ("vv", proj_last[0])]
            if which == "k":
                ws1.append(kout_free[1 - sl])
            E("vector", lambda v, kb=kb, bk=bk: v.tensor_copy(kb, bank(bk)),
              waits=ws1, inc="vv")
            e1 = P.tok("vv")
            E("vector", lambda v, rot=rot, kb=kb: v.stream_shuffle(rot, kb, SWAP32),
              waits=[("vv", e1)], inc="vv")
            e2 = P.tok("vv")
            E("vector", lambda v, tb_=tb_, rot=rot, stb=stb: v.tensor_mul(tb_, rot, stb),
              waits=[("vv", e2)], inc="vv")
            e3 = P.tok("vv")
            E("vector", lambda v, ub=ub, kb=kb, ctb=ctb: v.tensor_mul(ub, kb, ctb),
              waits=[("vv", e3)], inc="vv")
            e4 = P.tok("vv")
            bankfree[bk] = ("vv", e4)
            tb_free[sl] = ("vv", e4)
            if which == "q":
                E("vector", lambda v, out=out, ub=ub, tb_=tb_: v.tensor_add(out, ub, tb_),
                  waits=[("vv", e4)], inc="vv")
            else:
                E("vector", lambda v, out=out, ub=ub, tb_=tb_: v.tensor_add(out, ub, tb_),
                  waits=[("vv", e4)], inc="vv")
            proj_last[0] = P.tok("vv")
            if which == "k":
                if "k1dbg" in tap_ap and jc == 1 and tt == 0:
                    E("gpsimd", lambda g: g.dma_start(
                        out=tap_ap["k1dbg"][:, 2048:3072],
                        in_=s_att[:, 2048:3072]),
                        waits=[("vv", proj_last[0])], inc="lg", incn=16)
                if "k1dbg" in tap_ap and jc in (1, 3):
                    E("gpsimd", lambda g, jc=jc, tt=tt, sl=sl: g.dma_start(
                        out=tap_ap["k1dbg"][:, (jc // 2) * 1024 + tt * TT:
                                            (jc // 2) * 1024 + tt * TT + TT],
                        in_=v_kout[1 - sl]),
                        waits=[("vv", proj_last[0])], inc="lg", incn=16)
                E("sync", lambda s, jc=jc, tt=tt, sl=sl: s.dma_start(
                    out=d_kvs.ap()[:, jc * 1024 + tt * TT: jc * 1024 + tt * TT + TT],
                    in_=v_kout[1 - sl]),
                    waits=[("vv", proj_last[0])], inc="ls", incn=16)
                kout_free[1 - sl] = ("ls", P.tok("ls"))
        return proj_last[0]

    def proj_v(tt, xln_tok):
        def _vload(vblk):
            E("gpsimd", lambda g, vblk=vblk: g.dma_start(
                out=v_wqf[vblk % 2][:, 0:1024],
                in_=d_WV[vblk * 128:(vblk + 1) * 128, :]),
                waits=[wq_free[vblk % 2], ("vv", t_ffn1)], inc="lg", incn=16)
            return P.tok("lg")
        _vload(0)
        for half in range(2):
            stop_tok = {}
            for dc in range(NCH):
                sl = dc % 2
                wsl_ = v_wqb[sl]
                wfs = v_wqf[sl]
                vblk = half * NCH + dc
                tlwf = _vload(vblk + 1) if vblk + 1 < 2 * NCH else P.tok("lg")
                E("vector", lambda v, wsl_=wsl_, wfs=wfs: v.scalar_tensor_tensor(
                    wsl_[:, 0:1024], wfs[:, 0:1024], 1.0, wfs[:, 0:1024],
                    OP.mult, OP.max),
                    waits=[("lg", tlwf), wq_free[sl]], inc="vv")
                tlw = ("vv", P.tok("vv"))
                tok = None
                for tb in range(4):
                    for n in range(2):
                        bk = 2 * tb + n
                        ws = []
                        if tb == 0 and n == 0:
                            ws.append(tlw)
                        if dc == 0:
                            ws.append(bankfree[bk])
                            if tb == 0 and n == 0:
                                ws.append(("vv", xln_tok))
                        tok = E("tensor",
                                lambda t, wsl_=wsl_, dc=dc, tb=tb, n=n, bk=bk: t.matmul(
                            bank(bk),
                            s_xln[:, dc * TT + tb * 128: dc * TT + (tb + 1) * 128],
                            wsl_[:, n * 512:(n + 1) * 512],
                            start=(dc == 0), stop=(dc == NCH - 1)),
                            waits=ws, inc="mm")
                        if dc == NCH - 1:
                            stop_tok[bk] = tok
                wq_free[sl] = ("mm", tok)
            for tb in range(4):
                sl = tb % 2
                E("vector", lambda v, tb=tb, sl=sl: v.tensor_copy(
                    v_vev[sl], PS[:, tb * 1024:(tb + 1) * 1024]),
                    waits=[("mm", stop_tok[2 * tb + 1]), vev_free[sl]], inc="vv")
                tv0 = P.tok("vv")
                E("vector", lambda v, sl=sl: v.tensor_tensor(
                    v_vev[sl], v_vev[sl], v_vev[sl], OP.max),
                    waits=[("vv", tv0)], inc="vv")
                tv = P.tok("vv")
                bankfree[2 * tb] = bankfree[2 * tb + 1] = ("vv", tv)
                tbg = tt * 4 + tb
                E("sync", lambda s, tbg=tbg, half=half, sl=sl: s.dma_start(
                    out=d_kvs.ap()[:, 16384 + tbg * 2048 + half * 1024:
                                   16384 + tbg * 2048 + (half + 1) * 1024],
                    in_=v_vev[sl]), waits=[("vv", tv)], inc="ls", incn=16)
                vev_free[sl] = ("ls", P.tok("ls"))
        return P.tok("lg")

    t_ag_trigger = None
    tokl0 = ln_tile(0, extra_waits=[("vv", t_ffn1)])
    proj_qk("k", 0, tokl0)
    proj_v(0, tokl0)
    proj_qk("q", 0, tokl0)
    tokl1 = ln_tile(1)
    proj_qk("k", 1, tokl1)
    t_kv_done = proj_v(1, tokl1)
    t_q_done = proj_qk("q", 1, tokl1)

    E("gpsimd", lambda g: g.dma_start(out=s_kvix[:, :], in_=d_kvsel[:, :]),
      inc="lg", incn=16)
    t_ixl = P.tok("lg")
    E("gpsimd", lambda g: g.collective_compute(
        "AllGather", OP.bypass, replica_groups=[list(range(8))],
        ins=[d_kvs.ap().opt()], outs=[d_kvag.ap().opt()]),
        waits=[("lg", P.tok("lg")), ("ls", P.tok("ls")), ("sa", P.tok("sa")),
               ("vv", t_q_done)], inc="cc")
    for _ in range(1):
        E("gpsimd", lambda g: g.collective_compute(
            "AllReduce", OP.add, replica_groups=[list(range(8))],
            ins=[d_bin2.ap().opt()], outs=[d_bout2.ap().opt()]), inc="cc")
    t_cc2 = P.tok("cc")
    def _kv_snap(g):
        r = g.alloc_register("kvbase")
        g.reg_load(r, s_kvix[0:1, 0:1])
        sv = RuntimeValue(r, min_val=0, max_val=768)
        return g.dma_start(out=d_kvrb.ap()[:, :],
                           in_=d_kvag.ap()[bass.ds(sv, 256), :])
    E("gpsimd", _kv_snap, waits=[("cc", t_cc2), ("lg", t_ixl)], inc="lg", incn=16)
    t_kvag = ("lg", P.tok("lg"))

    if "kvs" in tap_ap:
        E("gpsimd", lambda g: g.dma_start(out=tap_ap["kvs"], in_=d_kvs.ap()),
          waits=[("lg", t_kv_done)], inc="lg", incn=16)
    if "xln" in tap_ap:
        E("gpsimd", lambda g: g.dma_start(out=tap_ap["xln"], in_=s_xln[:, :]),
          waits=[("vv", t_q_done)], inc="lg", incn=16)
    if "qf" in tap_ap:
        E("gpsimd", lambda g: g.dma_start(out=tap_ap["qf"], in_=s_U[:, 0:NCH * NTOK]),
          waits=[("vv", t_q_done)], inc="lg", incn=16)

    if cut == "proj":
        _finish_early(); _realize(); return nc

    def nq(j):
        return NTOK - 128 * j

    cb = {}
    off = 0
    for kh in range(2):
        for j in range(8):
            cb[(kh, j)] = off
            off += nq(j)
    assert off == 9216

    kb_free = [None, None]
    db_free = [None, None]
    vb_free = [None, None]
    att_tok = {}
    abank_n = [0]
    vb_n = [0]
    ab_n = [0]
    tp_free = [None] * 2
    avac_free = [None]

    def _kvrb_k_src(kh, j):
        base = kh * 128 * 32768 + (j // 4) * TT + (j % 4) * 128
        ap = d_kvrb.ap()
        return bass.AP(ap.tensor, base, [[32768, 128], [1024, NCH], [1, 128]])

    def phase_a(kh, j):
        sl = ab_n[0] % 2
        ab_n[0] += 1
        E("sync", lambda s, kh=kh, j=j, sl=sl: s.dma_start(
            out=s_kb[:, sl * 2048:(sl + 1) * 2048].rearrange("p (c u) -> p c u", c=NCH),
            in_=_kvrb_k_src(kh, j)),
            waits=[t_kvag, kb_free[sl]], inc="ls", incn=16)
        tlk = P.tok("ls")
        E("sync", lambda s, kh=kh, j=j, sl=sl: s.dma_start(
            out=s_db[:, sl * 1024:sl * 1024 + nq(j)],
            in_=d_dt[kh:kh + 1, j:j + 1, :, 128 * j:NTOK]),
            waits=[db_free[sl]], inc="ls", incn=16)
        tld = P.tok("ls")
        aslot = abank_n[0] % 2
        abank_n[0] += 1
        b0 = 2 * aslot
        nsg = (nq(j) + 511) // 512
        for sg in range(nsg):
            w = min(512, nq(j) - sg * 512)
            for dc in range(NCH):
                ws2 = []
                if dc == 0:
                    ws2 = [("ls", tlk), bankfree[b0 + sg]]
                    if sg == 0:
                        ws2.append(("vv", t_q_done))
                E("tensor", lambda t, sl=sl, dc=dc, sg=sg, w=w, j=j, b0=b0: t.matmul(
                    bank(b0 + sg, 0, w),
                    s_kb[:, sl * 2048 + dc * 128: sl * 2048 + (dc + 1) * 128],
                    s_U[:, dc * NTOK + 128 * j + sg * 512:
                         dc * NTOK + 128 * j + sg * 512 + w],
                    start=(dc == 0), stop=(dc == NCH - 1)), waits=ws2, inc="mm")
        tmm = P.tok("mm")
        kb_free[sl] = ("mm", tmm)
        E("vector", lambda v, kh=kh, j=j, sl=sl, b0=b0: v.tensor_tensor(
            s_att[:, cb[(kh, j)]:cb[(kh, j)] + nq(j)],
            PS[:, b0 * 512:b0 * 512 + nq(j)],
            s_db[:, sl * 1024:sl * 1024 + nq(j)], OP.mult),
            waits=[("mm", tmm), ("ls", tld)], inc="vv")
        tv = P.tok("vv")
        att_tok[(kh, j)] = tv
        db_free[sl] = ("vv", tv)
        for sg in range(nsg):
            bankfree[b0 + sg] = ("vv", tv)

    def phase_b(i):
        pairs = [(kh, j) for j in range(i + 1) for kh in range(2)]
        stop_tok = {}
        for pi, (kh, j) in enumerate(pairs):
            sl = vb_n[0] % 2
            vb_n[0] += 1
            E("sync", lambda s, kh=kh, j=j, sl=sl: s.dma_start(
                out=s_vb[:, sl * 2048:(sl + 1) * 2048],
                in_=d_kvrb.ap()[kh * 128:(kh + 1) * 128,
                                16384 + j * 2048:16384 + (j + 1) * 2048]),
                waits=[t_kvag, vb_free[sl]], inc="ls", incn=16)
            tlv = P.tok("ls")
            atile = s_att[:, cb[(kh, j)] + 128 * (i - j):
                          cb[(kh, j)] + 128 * (i - j) + 128]
            tok = None
            for n in range(4):
                bk = 4 + n
                ws2 = [("vv", att_tok[(kh, j)])]
                if n == 0:
                    ws2.append(("ls", tlv))
                if pi == 0:
                    ws2.append(bankfree[bk])
                tok = E("tensor", lambda t, atile=atile, sl=sl, n=n, bk=bk,
                        st_=(pi == 0), sp_=(pi == len(pairs) - 1): t.matmul(
                    bank(bk), atile,
                    s_vb[:, sl * 2048 + n * 512: sl * 2048 + (n + 1) * 512],
                    start=st_, stop=sp_),
                    waits=ws2, inc="mm")
                if pi == len(pairs) - 1:
                    stop_tok[bk] = tok
            vb_free[sl] = ("mm", tok)
        E("vector", lambda v: v.tensor_copy(s_S3, PS[:, 2048:4096]),
          waits=[("mm", stop_tok[7]), avac_free[0]], inc="vv")
        tv = P.tok("vv")
        for bk in range(4, 8):
            bankfree[bk] = ("vv", tv)
        aslot = (abank_n[0] + 1) % 2
        b0 = 2 * aslot
        tadd = tv
        for dc in range(NCH):
            bsel = b0 + (dc % 2)
            q = (dc // 2) % 2
            tp = PS[:, bsel * 512 + q * 128: bsel * 512 + (q + 1) * 128]
            ws = [("vv", tv), tp_free[dc % 2]]
            if dc < 2:
                ws += [bankfree[bsel]]
            E("tensor", lambda t, tp=tp, dc=dc: t.transpose(
                tp, s_S3[:, dc * 128:(dc + 1) * 128], s_id[:, :]),
                waits=ws, inc="mm")
            tm = P.tok("mm")
            xsl = s_xf[:, dc * NTOK + i * 128: dc * NTOK + (i + 1) * 128]
            E("vector", lambda v, xsl=xsl, tp=tp: v.tensor_add(xsl, xsl, tp),
              waits=[("mm", tm)], inc="vv")
            tp_free[dc % 2] = ("vv", P.tok("vv"))
            tadd = P.tok("vv")
        bankfree[b0] = bankfree[b0 + 1] = ("vv", tadd)
        avac_free[0] = ("vv", tadd)
        return tadd

    t_attn = 0
    for j in range(8):
        phase_a(0, j)
        phase_a(1, j)
        t_attn = phase_b(j)

    if "kvrb" in tap_ap:
        E("gpsimd", lambda g: g.dma_start(out=tap_ap["kvrb"], in_=d_kvrb.ap()),
          waits=[t_kvag], inc="lg", incn=16)
    if "att" in tap_ap:
        E("gpsimd", lambda g: g.dma_start(out=tap_ap["att"], in_=s_att[:, :]),
          waits=[("vv", t_attn)], inc="lg", incn=16)
    if "x2" in tap_ap:
        E("gpsimd", lambda g: g.dma_start(out=tap_ap["x2"], in_=s_xf[:, :]),
          waits=[("vv", t_attn)], inc="lg", incn=16)

    if cut == "attn":
        _finish_early(); _realize(); return nc

    for tt in range(NT):
        tokl = ln_tile(tt, extra_waits=[("vv", t_attn)])
        ffn(1, tt, tokl)
    n_st = P.tok("sa")
    E("scalar", lambda a: a.nop(), waits=[("sa", n_st)])
    _realize()
    return nc


_NC_CACHE = {}


def _get_nc(taps=()):
    key = (tuple(taps), _CUT, _QWIN)
    if key not in _NC_CACHE:
        _NC_CACHE[key] = _build_nc(taps, _CUT, _QWIN)
    return _NC_CACHE[key]


def _wqk_blocks(W):
    return np.ascontiguousarray(
        W.reshape(16, 128, 16, 128).transpose(2, 1, 0, 3).reshape(H, H))


def _wv_blocks(W):
    return np.ascontiguousarray(
        W.reshape(16, 128, 2, 1024).transpose(2, 0, 1, 3).reshape(2 * H, H // 2))


def _tbl_blocks(t):
    t = np.ascontiguousarray(t, np.float32)
    return np.ascontiguousarray(
        t.reshape(16, 128, 2, TT).transpose(0, 2, 1, 3).reshape(2 * H, TT))


def _make_in_maps(L0_attn_X, L0_attn_W_Q, L0_attn_W_K, L0_attn_W_V,
                  L0_ffn_w1, L0_ffn_w2, L0_ffn_w3):
    X = np.asarray(L0_attn_X, dtype=np.float32)
    WQ = np.ascontiguousarray(np.asarray(L0_attn_W_Q, dtype=np.float32))
    WK = np.ascontiguousarray(np.asarray(L0_attn_W_K, dtype=np.float32))
    WV = np.ascontiguousarray(np.asarray(L0_attn_W_V, dtype=np.float32))
    w1T = np.ascontiguousarray(np.asarray(L0_ffn_w1, np.float32).T)
    w2T = np.ascontiguousarray(np.asarray(L0_ffn_w2, np.float32).T)
    w3T = np.ascontiguousarray(np.asarray(L0_ffn_w3, np.float32).T)
    cqT, sqT, ckT, skT, D = _host_tables()
    ident = np.eye(128, dtype=np.float32)

    w_fl = [w1T.reshape(8, 128, WSL), w2T.reshape(8, 128, WSL),
            w3T.reshape(8, 128, WSL)]

    in_maps = []
    for c in range(8):
        b, par = c // 2, c % 2
        xT = np.ascontiguousarray(X[b, par::2, :].T)
        Dq = D[par::2, :]
        dtt = np.empty((2, 8, 128, NTOK), np.float32)
        for kh in range(2):
            dtt[kh] = np.ascontiguousarray(Dq[:, kh::2].T).reshape(8, 128, NTOK)
        m = {
            "xT": xT,
            "w0sl": np.ascontiguousarray(w_fl[0][c]),
            "w1sl": np.ascontiguousarray(w_fl[1][c]),
            "w2sl": np.ascontiguousarray(w_fl[2][c]),
            "WQ": _wqk_blocks(WQ), "WK": _wqk_blocks(WK),
            "WV": _wv_blocks(WV),
            "cq": _tbl_blocks(cqT[:, par::2]),
            "sq": _tbl_blocks(sqT[:, par::2]),
            "ck": _tbl_blocks(ckT[:, par::2]),
            "sk": _tbl_blocks(skT[:, par::2]),
            "dtt": dtt,
            "ident": ident,
            "kv_sel": np.array([[256 * b]], np.int32),
        }
        if _QWIN:
            import ml_dtypes
            for wi, wT in enumerate((w1T, w2T, w3T)):
                g = np.abs(wT).mean(dtype=np.float64)
                q = np.clip(np.round(wT / (np.float32(g) + np.float32(EPS_Q))),
                            -1, 1).astype(ml_dtypes.bfloat16)
                m[f"q{wi}win"] = q
        in_maps.append(m)
    return in_maps


def kernel(L0_attn_X, L0_attn_W_Q, L0_attn_W_K, L0_attn_W_V,
           L0_ffn_w1, L0_ffn_w2, L0_ffn_w3):
    in_maps = _make_in_maps(L0_attn_X, L0_attn_W_Q, L0_attn_W_K, L0_attn_W_V,
                            L0_ffn_w1, L0_ffn_w2, L0_ffn_w3)
    nc = _get_nc(_TAPS)
    res = run_bass_kernel_spmd(nc, in_maps, core_ids=list(range(8)))

    out = np.empty((B, S, H), np.float32)
    for c in range(8):
        b, par = c // 2, c % 2
        out[b, par::2, :] = res.results[c]["y"].T
    if _TAPS:
        kernel._tap_results = res.results
    return out
